# version 36
# baseline (speedup 1.0000x reference)
import dataclasses
import numpy as np
import ml_dtypes
from contextlib import ExitStack

import concourse.bass as bass
import concourse.mybir as mybir
from concourse.bass_utils import run_bass_kernel_spmd

F32 = mybir.dt.float32
BF16 = mybir.dt.bfloat16
U32 = mybir.dt.uint32
AF = mybir.ActivationFunctionType
ALU = mybir.AluOpType
AXX = mybir.AxisListType.X
bfdt = ml_dtypes.bfloat16

B, T, E = 2, 8192, 1024
H, dk, V, dv = 8, 128, 2048, 256
CH = 128
TLOC = 2048
NCH = TLOC // CH
IF = 2816
NCORE = 8
M_FIX = float(np.float32(1.5 * 2 ** 23))
K_VELT = float(np.float32(2 ** 22 + 1))
LN_EPS = 1e-5
GN_EPS = 1e-5
SH_GU = 352
SH_D = 128
SHARD_ELEMS = SH_GU * 1024 * 2 + SH_D * IF
WU_OFF = SH_GU * 1024
WD_OFF = SH_GU * 1024 * 2


class Res:
    __slots__ = ("name", "last_w", "readers")

    def __init__(self, name, last_w=()):
        self.name = name
        self.last_w = list(last_w)
        self.readers = []


class _DmaProxy:

    def __init__(self, h, incs):
        self._h = h
        self._incs = list(incs)
        self._i = 0

    def __getattr__(self, a):
        return getattr(self._h, a)

    def dma_start(self, **kw):
        assert self._i < len(self._incs), "stage issued more DMAs than n_dma"
        sem = self._incs[self._i]
        self._i += 1
        inst = self._h.dma_start(**kw)
        inst.then_inc(sem, 16)
        return inst


class KB:

    ENGS = ("tensor", "vector", "scalar", "gpsimd", "sync")
    NLANES = {"sync": 8, "gpsimd": 4, "scalar": 2, "vector": 1, "tensor": 1}

    def __init__(self, nc, ctx, block, n_cc=2):
        self.nc = nc
        self.block = block
        self.sems = {}
        self.cnt = {}
        self.hwm = {}
        self.lanesems = {}
        self.lane_rr = {}
        self.nlanes = {}
        self.pending = {e: [] for e in self.ENGS}
        for e in self.ENGS:
            self.sems[e] = ctx.enter_context(nc.semaphore(f"kb_{e}"))
            self.cnt[e] = 0
            self.hwm[e] = {}
            nl = self.NLANES[e]
            self.nlanes[e] = nl
            self.lane_rr[e] = 0
            self.lanesems[e] = []
            for i in range(nl):
                sm = ctx.enter_context(nc.semaphore(f"kb_{e}_l{i}"))
                self.lanesems[e].append(sm)
                self.sems[f"{e}.l{i}"] = sm
                self.cnt[f"{e}.l{i}"] = 0
        self.ccsems = [ctx.enter_context(nc.semaphore(f"kb_cc{i}"))
                       for i in range(n_cc)]
        for i in range(n_cc):
            self.sems[f"cc{i}"] = self.ccsems[i]
            self.cnt[f"cc{i}"] = 0
        self.cc_idx = 0

    def _deps(self, reads, writes, extra):
        deps = {}

        def add(tok):
            if tok is None:
                return
            sm, v = tok
            if deps.get(sm, -1) < v:
                deps[sm] = v

        for r in reads:
            for t in r.last_w:
                add(t)
        for w in writes:
            for t in w.last_w:
                add(t)
            for t in w.readers:
                add(t)
        for t in extra:
            add(t)
        return deps

    def _wait_list(self, eng, deps):
        hwm = self.hwm[eng]
        waits = []
        for sname in sorted(deps):
            v = deps[sname]
            if sname == eng == "tensor":
                continue
            if hwm.get(sname, 0) >= v:
                continue
            waits.append((self.sems[sname], v))
            hwm[sname] = v
        return waits

    def stage(self, eng, fn, reads=(), writes=(), extra=(), dma=False,
              n_dma=1, cc=False):
        deps = self._deps(reads, writes, extra)
        waits = self._wait_list(eng, deps)
        if dma:
            incs = []
            toks = []
            hwm = self.hwm[eng]
            nl = self.nlanes[eng]
            for _ in range(n_dma):
                lane = self.lane_rr[eng] % nl
                self.lane_rr[eng] += 1
                lname = f"{eng}.l{lane}"
                prev = self.cnt[lname]
                if prev > 0 and hwm.get(lname, 0) < prev:
                    waits.append((self.sems[lname], prev))
                    hwm[lname] = prev
                self.cnt[lname] = prev + 16
                incs.append(self.lanesems[eng][lane])
                toks.append((lname, prev + 16))
            self.pending[eng].append(("dma", waits, fn, incs, n_dma))
        elif cc:
            i = self.cc_idx
            self.cc_idx += 1
            self.cnt[f"cc{i}"] = 1
            toks = [(f"cc{i}", 1)]
            self.pending[eng].append(("cc", waits, fn, self.ccsems[i], 1))
        else:
            self.cnt[eng] += 1
            toks = [(eng, self.cnt[eng])]
            self.pending[eng].append(("op", waits, fn, self.sems[eng], 1))
        for r in reads:
            r.readers.extend(toks)
        for w in writes:
            w.last_w = list(toks)
            w.readers = []
        return toks

    def barrier(self):
        toks = [(sname, v) for sname, v in self.cnt.items()
                if v > 0 and not sname.startswith("cc")]
        for e in self.ENGS:
            deps = {}
            for sname, v in toks:
                if sname == e:
                    continue
                deps[sname] = v
            waits = self._wait_list(e, deps)
            if waits:
                self.pending[e].append(("wait", waits, None, None, 0))
        return toks

    def finalize(self):
        for e in self.ENGS:
            entries = self.pending[e]

            def play(h, _entries=entries):
                for kind, waits, fn, sem, n in _entries:
                    for sm, v in waits:
                        h.wait_ge(sm, v)
                    if kind == "wait":
                        continue
                    if kind == "dma":
                        proxy = _DmaProxy(h, sem)
                        fn(proxy)
                        assert proxy._i == n, \
                            f"dma stage issued {proxy._i} != declared {n}"
                    elif kind == "cc":
                        inst = fn(h)
                        inst.then_inc(sem, 1)
                    else:
                        inst = fn(h)
                        assert inst is not None
                        inst.then_inc(sem, 1)
            getattr(self.block, e)(play)
        self.pending = {e: [] for e in self.ENGS}


def _gammas():
    s_, e_ = np.log(1 / 32.0), np.log(1 / 512.0)
    return 1.0 - np.exp(np.linspace(s_, e_, H))


GAM_PY = _gammas()


def _perm_qk():
    return np.concatenate(
        [np.concatenate([128 * h + np.arange(0, 128, 2),
                         128 * h + np.arange(1, 128, 2)]) for h in range(H)])


def _cs_tables(pos, gam):
    D = 1024
    half = 512
    fidx = np.arange(half)
    scale_vec = (2 * fidx + 0.4 * D) / (1.4 * D)
    scale = scale_vec[None, :] ** (pos[:, None] / 512.0)
    inv_freq = 1.0 / 10000.0 ** (fidx / half)
    sinu = pos[:, None] * inv_freq[None, :]
    sin, cos = np.sin(sinu), np.cos(sinu)
    cq, sq = cos * scale, sin * scale
    ck, sk = cos / scale, sin / scale
    tau = (pos.astype(np.int64) % CH).astype(np.float64)
    dec = np.zeros((len(pos), half))
    for h in range(H):
        dec[:, 64 * h:64 * (h + 1)] = (gam[h] ** (127 - tau))[:, None]
    return (cq.astype(np.float32), sq.astype(np.float32),
            (ck * dec).astype(np.float32), (sk * dec).astype(np.float32))


def _host_consts():
    gam = GAM_PY
    tau = np.arange(CH, dtype=np.float64)
    qddec = np.zeros((CH, 1024))
    dtp = np.zeros((CH, H * CH))
    gci = np.zeros((CH, H * CH))
    eye = np.eye(CH)
    tq = np.arange(CH)[None, :]
    tk = np.arange(CH)[:, None]
    for h in range(H):
        qddec[:, 128 * h:128 * (h + 1)] = (gam[h] ** (tau + 1))[:, None]
        dtp[:, 128 * h:128 * (h + 1)] = (tq >= tk) * gam[h] ** (tq - 127.0)
        gci[:, 128 * h:128 * (h + 1)] = eye * (gam[h] ** CH)
    oi = np.zeros((128, 256), dtype=np.float32)
    oi[:, 0:128] = 1.0
    oi[:, 128:256] = np.eye(128)
    return dict(gam=gam, qddec=qddec.astype(bfdt), dtp=dtp.astype(bfdt),
                gci=gci.astype(bfdt), oi=oi.astype(bfdt))


def _core_consts(core, gam):
    bi, ci = core // 4, core % 4
    t0 = ci * TLOC
    pos = np.arange(t0, t0 + TLOC, dtype=np.float64)
    cq, sq, ckf, skf = _cs_tables(pos, gam)
    cs = np.concatenate([cq, sq, ckf, skf], axis=1).astype(bfdt)
    coef = np.zeros((8, 8))
    for r in range(8):
        if r // 4 == bi and (r % 4) < ci:
            for h in range(H):
                coef[r, h] = gam[h] ** (TLOC * (ci - 1 - (r % 4)))
    coef_bc = np.tile(coef.reshape(1, 64), (128, 1)).astype(np.float32)
    return dict(cs=cs, coef=coef_bc, bi=bi, t0=t0)


def _dram_view(t, offset, dims):
    ap = t.ap()
    return dataclasses.replace(ap, offset=offset, ap=[[s, n] for s, n in dims])


def quant_ops(P, n, x_ap, x_res, out_ap, out_res, sc, pool=(),
              sign_act=False, abs_dve=False):
    nb = n // 16

    def b3(ap):
        return ap.rearrange("p (b k) -> p b k", k=16)

    def bc3(ap):
        return ap.rearrange("p (b o) -> p b o", o=1).broadcast_to((P, nb, 16))

    am, amR = sc["am"]
    sc6, scR = sc["sc6"]
    rs6, rsR = sc["rs6"]
    am = am[0:P, 0:nb]
    sc6 = sc6[0:P, 0:nb]
    rs6 = rs6[0:P, 0:nb]
    ax, axR = sc["ax"]
    un, unR = sc["un"]
    M2, M2R = sc["M2"]
    sb, sbR = sc["sb"]
    ax = ax[0:P, 0:n]
    un = un[0:P, 0:n]
    M2 = M2[0:P, 0:n]
    sb = sb[0:P, 0:n]

    ops = []
    ops.append(("vector", lambda v: v.tensor_reduce(
        am, b3(x_ap), AXX, ALU.max, apply_absolute_value=True),
        [x_res], [amR]))
    ops.append(("vector", lambda v: v.tensor_scalar(
        sc6, am, 6e-12, 1.0 / 12.0, ALU.max, ALU.mult), [amR], [scR]))
    ops.append(("vector", lambda v: v.reciprocal(rs6, sc6), [scR], [rsR]))
    if abs_dve:
        ops.append(("vector", lambda v: v.tensor_scalar(
            ax.bitcast(U32), x_ap.bitcast(U32), 0x7FFFFFFF, None,
            ALU.bitwise_and), [x_res], [axR]))
    else:
        ops.append(("scalar", lambda a: a.activation(
            out=ax, in_=x_ap, func=AF.Abs), [x_res], [axR]))
    if sign_act:
        ops.append(("scalar", lambda a: a.activation(
            out=sb, in_=x_ap, func=AF.Sign), [x_res], [sbR]))
    else:
        ops.append(("vector", lambda v: v.tensor_scalar(
            sb.bitcast(U32), x_ap.bitcast(U32), 0x80000000, None,
            ALU.bitwise_and), [x_res], [sbR]))
    ops.append(("gpsimd" if "un" in pool else "vector",
                lambda v: v.tensor_tensor(
        b3(un), b3(ax), bc3(rs6), ALU.mult), [axR, rsR], [unR]))
    ops.append(("vector", lambda v: v.tensor_scalar(
        ax, un, 4.0, M_FIX, ALU.min, ALU.add), [unR], [axR]))
    ops.append(("vector", lambda v: v.tensor_scalar(
        M2, un, 4.0, None, ALU.max), [unR], [M2R]))
    ops.append(("vector", lambda v: v.tensor_scalar(
        ax, ax, -(M_FIX + 4.0), None, ALU.add), [axR], [axR]))
    ops.append(("vector", lambda v: v.tensor_scalar(
        un, M2, K_VELT, None, ALU.mult), [M2R], [unR]))
    ops.append(("vector", lambda v: v.tensor_tensor(
        M2, un, M2, ALU.subtract), [unR, M2R], [M2R]))
    ops.append(("vector", lambda v: v.tensor_tensor(
        un, un, M2, ALU.subtract), [unR, M2R], [unR]))
    ops.append(("vector", lambda v: v.tensor_tensor(
        ax, ax, un, ALU.add), [axR, unR], [axR]))
    if sign_act:
        ops.append(("gpsimd" if "ss" in pool else "vector",
                    lambda v: v.tensor_tensor(
            b3(sb), b3(sb), bc3(sc6), ALU.mult), [sbR, scR], [sbR]))
    else:
        ops.append(("vector", lambda v: v.tensor_tensor(
            b3(sb).bitcast(U32), b3(sb).bitcast(U32),
            bc3(sc6).bitcast(U32), ALU.bitwise_or), [sbR, scR], [sbR]))
    ops.append(("gpsimd" if "q" in pool else "vector",
                lambda v: v.tensor_tensor(
        out_ap, ax, sb, ALU.mult), [axR, sbR], [out_res]))
    return ops


def emit_quant_multi(kb, calls):
    lists = [quant_ops(*c) for c in calls]
    for i in range(max(len(lst) for lst in lists)):
        for lst in lists:
            if i < len(lst):
                eng, fn, rd, wr = lst[i]
                kb.stage(eng, fn, reads=rd, writes=wr)


class KQ:

    def __init__(self):
        self.calls = []

    def stage(self, eng, fn, reads=(), writes=(), extra=(), dma=False,
              n_dma=1, cc=False):
        self.calls.append((eng, fn, reads, writes, extra, dma, n_dma, cc))


def kq_merge(kb, queues):
    n = max((len(q.calls) for q in queues), default=0)
    for i in range(n):
        for q in queues:
            if i < len(q.calls):
                eng, fn, rd, wr, ex, dma, n_dma, cc = q.calls[i]
                kb.stage(eng, fn, reads=rd, writes=wr, extra=ex, dma=dma,
                         n_dma=n_dma, cc=cc)


RSQRT_MAGIC = float(np.uint32(0x5F3759DF).view(np.float32))


def emit_rsqrt(kb, w_ap, w_res, out_ap, out_res, ta, taR, tb, tbR, mg, mgR,
               n, iters=2):
    kb.stage("vector", lambda v: v.tensor_scalar(
        ta.bitcast(U32), w_ap.bitcast(U32), 1, None, ALU.logical_shift_right),
        reads=[w_res], writes=[taR])
    kb.stage("vector", lambda v: v.tensor_tensor(
        ta.bitcast(U32), mg.bitcast(U32).broadcast_to((128, n)),
        ta.bitcast(U32), ALU.subtract),
        reads=[taR, mgR], writes=[taR])
    for it in range(iters):
        last = it == iters - 1
        kb.stage("vector", lambda v: v.tensor_tensor(tb, ta, ta, ALU.mult),
                 reads=[taR], writes=[tbR])
        kb.stage("vector", lambda v: v.tensor_tensor(tb, tb, w_ap, ALU.mult),
                 reads=[tbR, w_res], writes=[tbR])
        kb.stage("vector", lambda v: v.tensor_scalar(
            tb, tb, -0.5, 1.5, ALU.mult, ALU.add),
            reads=[tbR], writes=[tbR])
        dst, dstR = (out_ap, out_res) if last else (ta, taR)
        kb.stage("vector", lambda v, dst=dst: v.tensor_tensor(
            dst, ta, tb, ALU.mult),
            reads=[taR, tbR], writes=[dstR])
    return out_res


def build_nc(dbg=False, stop_after=9, nch1=NCH, lite0=False):
    nc = bass.Bass("TRN2", target_bir_lowering=False, debug=False)

    xsl = nc.dram_tensor("xsl", [TLOC, E], F32, kind="ExternalInput")
    cs_in = nc.dram_tensor("cs_in", [TLOC, 2048], BF16, kind="ExternalInput")
    wqkvp = nc.dram_tensor("wqkvp", [E, 4096], F32, kind="ExternalInput")
    wg_in = nc.dram_tensor("wg_in", [E, 2048], F32, kind="ExternalInput")
    wp_in = nc.dram_tensor("wp_in", [V, E], F32, kind="ExternalInput")
    wgate = nc.dram_tensor("wgate", [SH_GU, E], F32, kind="ExternalInput")
    wup = nc.dram_tensor("wup", [SH_GU, E], F32, kind="ExternalInput")
    wdown = nc.dram_tensor("wdown", [SH_D, IF], F32, kind="ExternalInput")
    bia_in = nc.dram_tensor("bia_in", [1, 7168], F32, kind="ExternalInput")
    dtp_in = nc.dram_tensor("dtp_in", [128, 1024], BF16, kind="ExternalInput")
    qdd_in = nc.dram_tensor("qdd_in", [128, 1024], BF16, kind="ExternalInput")
    gci_in = nc.dram_tensor("gci_in", [128, 1024], BF16, kind="ExternalInput")
    coef_in = nc.dram_tensor("coef_in", [128, 64], F32, kind="ExternalInput")
    oi_in = nc.dram_tensor("oi_in", [128, 256], BF16, kind="ExternalInput")

    y_out = nc.dram_tensor("y", [TLOC, E], F32, kind="ExternalOutput")

    ret_scr = nc.dram_tensor("ret_scr", [TLOC, V], F32)
    x2_scr = nc.dram_tensor("x2_scr", [TLOC, E], F32)
    x2qt_scr = nc.dram_tensor("x2qt_scr", [TLOC, E], BF16)
    qdt_scr = nc.dram_tensor("qdt_scr", [TLOC, E], BF16)
    xfms_scr = nc.dram_tensor("xfms_scr", [TLOC, E], BF16)
    sg_scr = nc.dram_tensor("sg_scr", [TLOC, V], BF16)
    wsh_in = nc.dram_tensor("wsh_in", [SHARD_ELEMS // 128, 128], BF16)
    wsh_all = nc.dram_tensor("wsh_all", [NCORE * SHARD_ELEMS // 128, 128], BF16)
    st_in = nc.dram_tensor("st_in", [128, 2048], BF16)
    st_all = nc.dram_tensor("st_all", [NCORE * 128, 2048], BF16)

    dbg_outs = {}
    if dbg:
        for nm, sh in [("d_qkv", [128, 4096]), ("d_P", [128, 2048]),
                       ("d_sext", [128, 2048]), ("d_ret", [128, 2048]),
                       ("d_g", [128, 2048]), ("d_x2", [128, 1024]),
                       ("d_x2q", [128, 1024]), ("d_sg", [128, 2048])]:
            dbg_outs[nm] = nc.dram_tensor(nm, sh, F32, kind="ExternalOutput")

    with ExitStack() as ctx:
        ent = ctx.enter_context

        T1 = nc.alloc_sbuf_tensor("T1", [128, 32768], BF16)
        T2 = nc.alloc_sbuf_tensor("T2", [128, 16384], BF16)
        A5 = nc.alloc_sbuf_tensor("A5", [128, 15360], BF16)
        A6 = nc.alloc_sbuf_tensor("A6", [128, 12288], BF16)
        QS = nc.alloc_sbuf_tensor("QS", [128, 4 * 1024 + 3 * 88], F32)
        SS = nc.alloc_sbuf_tensor("SS", [128, 1536], F32)
        SSB = nc.alloc_sbuf_tensor("SSB", [128, 128], BF16)
        MG = nc.alloc_sbuf_tensor("MG", [128, 1], F32)
        CQ = nc.alloc_sbuf_tensor("CQ", [128, 8], F32)
        W0 = nc.alloc_sbuf_tensor("W0", [128, 1408], F32)
        W0Q = nc.alloc_sbuf_tensor("W0Q", [128, 1024], BF16)
        W0S = nc.alloc_sbuf_tensor("W0S", [128, 1024], BF16)
        DTP = nc.alloc_sbuf_tensor("DTP", [128, 1024], BF16)
        QDD = nc.alloc_sbuf_tensor("QDD", [128, 1024], BF16)
        GCI = nc.alloc_sbuf_tensor("GCI", [128, 1024], BF16)
        COEF = nc.alloc_sbuf_tensor("COEF", [128, 64], F32)
        OI = nc.alloc_sbuf_tensor("OI", [128, 256], BF16)
        NB2 = nc.alloc_sbuf_tensor("NB2", [2, 2048], BF16)
        CB2 = nc.alloc_sbuf_tensor("CB2", [2, 7168], BF16)
        assert nc.sbuf_bytes_remaining >= 0

        psq = nc.alloc_psum_tensor("psq", [128, 1024], F32)
        psk = nc.alloc_psum_tensor("psk", [128, 1024], F32)
        pst = nc.alloc_psum_tensor("pst", [128, 2048], BF16)
        ps0 = nc.alloc_psum_tensor("ps0", [128, 512], F32)
        ps1 = nc.alloc_psum_tensor("ps1", [128, 512], F32)

        block = ent(nc.Block())
        kb = KB(nc, ctx, block)
        R = Res

        def bfv(t, c0, n):
            return t.ap()[:, c0:c0 + n]

        def f32v(t, c0, n):
            return t.ap()[:, c0:c0 + 2 * n].bitcast(F32)

        wqkv_t = [bfv(T1, kc * 4096, 4096) for kc in range(8)]
        wqkv_R = R("wqkv")
        retp_v = [f32v(T1, 0, 2048), f32v(T1, 4096, 2048)]
        retp_R = [R("retp0"), R("retp1")]
        g_v, g_R = f32v(T1, 8192, 2048), R("g")
        xc_v = [f32v(T1, 12288, 1024), f32v(T1, 14336, 1024)]
        xc_R = [R("xc0"), R("xc1")]
        x2_v, x2_R = f32v(T1, 16384, 1024), R("x2")
        sgin_v = [bfv(T1, 20480, 2048), bfv(T1, 22528, 2048)]
        sgin_R = [R("sgin0"), R("sgin1")]
        norm_v, norm_R = bfv(T1, 24576, 2048), R("normed")
        gated_v, gated_R = bfv(T1, 26624, 2048), R("gated")
        gatT_v, gatT_R = bfv(T1, 28672, 2048), R("gatedT")
        x2q_v, x2q_R = bfv(T1, 30720, 1024), R("x2q")
        qdin_v, qdin_R = bfv(T1, 31744, 1024), R("qdin")
        hqt = T1.ap()
        hqt_R = R("hqt")
        y_v, y_R = f32v(T1, 22528, 1024), R("ysb")
        x2r_v = [f32v(T1, 24576, 1024), f32v(T1, 26624, 1024)]
        x2r_R = [R("x2r0"), R("x2r1")]
        hq_v = [bfv(T1, 28672, 1408), bfv(T1, 30080, 1408)]
        hq_R = [R("hq0"), R("hq1")]

        wg_t = [bfv(T2, kc * 2048, 2048) for kc in range(8)]
        wg_R = R("wg")
        wp_R = R("wp")
        wp_t = ([bfv(A6, kc * 1024, 1024) for kc in range(4)]
                + [bfv(A6, 6144 + kc * 1024, 1024) for kc in range(4)]
                + [DTP.ap()[:, :], QDD.ap()[:, :], GCI.ap()[:, :]]
                + [W0.ap().bitcast(BF16)[:, 0:1024],
                   W0.ap().bitcast(BF16)[:, 1024:2048],
                   W0Q.ap()[:, :], W0S.ap()[:, :],
                   bfv(A5, 14336, 1024)])
        wf_v, wf_R = bfv(T2, 0, 11264), wg_R

        cs_v = [bfv(A5, 0, 2048), bfv(A5, 2048, 2048)]
        cs_R = [R("cs0"), R("cs1")]
        xb_v = [bfv(A5, 4096, 1024), bfv(A5, 5120, 1024)]
        xb_R = [R("xb0"), R("xb1")]
        xfc_v = [bfv(A5, 6144, 1024), bfv(A5, 7168, 1024)]
        xfc_R = [R("xfc0"), R("xfc1")]
        xfm_v, xfm_R = bfv(A5, 8192, 1024), R("xfm")
        xsq_v, xsq_R = bfv(A5, 9216, 1024), R("xsq")
        qsb_v, qsb_R = bfv(A5, 10240, 1024), R("qsb")
        ksb_v, ksb_R = bfv(A5, 11264, 1024), R("ksb")
        qpr_v, qpr_R = bfv(A5, 12288, 1024), R("qpr")
        kdc_v, kdc_R = bfv(A5, 13312, 1024), R("kdc")
        qdt_v, qdt_R = bfv(A5, 14336, 1024), R("qdtok")
        qfm_v, qfm_R = xfm_v, xfm_R
        kfm_v, kfm_R = xsq_v, xsq_R
        scr_v, scr_R = bfv(A5, 10240, 1024), qsb_R
        qdstg_v, qdstg_R = bfv(A5, 11264, 1024), ksb_R
        x2qt_v, x2qt_R = bfv(A5, 0, 8192), R("x2qtstage")

        vc_v, vc_R = bfv(A6, 0, 2048), R("vc")
        P_v = bfv(A6, 2048, 2048)
        P_half = [P_v[:, 0:1024], P_v[:, 1024:2048]]
        P_R = [R("P0"), R("P1")]
        sext_v, sext_R = bfv(A6, 4096, 2048), R("sext")
        ret_v, ret_R = f32v(A6, 6144, 2048), R("retsb")
        sgtmp_v, sgtmp_R = bfv(A6, 0, 2048), R("sgtmp")
        sgstg_v, sgstg_R = bfv(A6, 10240, 2048), R("sgstg")
        sgb_v, sgb_R = bfv(A6, 0, 11264), R("sgb")

        qsap = {}
        off = 0
        for nm, n in [("ax", 1024), ("un", 1024), ("M2", 1024), ("sb", 1024),
                      ("am", 88), ("sc6", 88), ("rs6", 88)]:
            qsap[nm] = QS.ap()[:, off:off + n]
            off += n

        def _mk_qs(tag, c0, c1, b0, b1):
            d = {}
            for nm in ("ax", "un", "M2", "sb"):
                d[nm] = (qsap[nm][:, c0:c1], R(tag + "_" + nm))
            for nm in ("am", "sc6", "rs6"):
                d[nm] = (qsap[nm][:, b0:b1], R(tag + "_" + nm))
            return d
        qsA = _mk_qs("qA", 0, 512, 0, 32)
        qsB = _mk_qs("qB", 512, 1024, 32, 64)
        t2f = T2.ap().bitcast(F32)
        qs3 = {}
        o3 = 5632
        for nm, n in [("ax", 384), ("un", 384), ("M2", 384), ("sb", 384),
                      ("am", 24), ("sc6", 24), ("rs6", 24)]:
            qs3[nm] = (t2f[:, o3:o3 + n], R("q3_" + nm))
            o3 += n

        ssv = {}
        off = 0
        for nm, n in [("tmp1", 512), ("tmp2", 512), ("stm", 128), ("stv", 128),
                      ("smsq", 128), ("gn", 64), ("ln2", 64)]:
            ssv[nm] = (SS.ap()[:, off:off + n], R("s_" + nm))
            off += n
        rstd_v, rstd_R = SSB.ap()[:, 0:128], R("rstd")
        mg_v, mg_R = MG.ap()[:, 0:1], R("magic")

        nb2_R = R("nb2")
        cb2_R = R("cb2")
        ones_col = OI.ap()[:, 0:1]
        ident = OI.ap()[:, 128:256]
        oi_R = R("oi")
        con_R = R("consts")
        dtp_R = R("dtp")
        qdd_R = R("qdd")
        gci_R = R("gci")

        psq_R = [R("psq0"), R("psq1")]
        psk_R = [R("psk0"), R("psk1")]
        pst_R = [R("pstA"), R("pstB")]
        ps0_R = R("ps0")
        ps1_R = R("ps1")

        w0_R, w0q_R, w0s_R = R("w0"), R("w0q"), R("w0s")
        wsh_R, wshall_R = R("wsh"), R("wshall")
        stin_R, stall_R = R("stin"), R("stall")

        kb.stage("sync", lambda h: [
            h.dma_start(out=DTP.ap()[:, :], in_=dtp_in.ap()[:, :]),
            h.dma_start(out=QDD.ap()[:, :], in_=qdd_in.ap()[:, :]),
            h.dma_start(out=GCI.ap()[:, :], in_=gci_in.ap()[:, :]),
            h.dma_start(out=COEF.ap()[:, :], in_=coef_in.ap()[:, :]),
            h.dma_start(out=OI.ap()[:, :], in_=oi_in.ap()[:, :]),
        ], writes=[con_R, dtp_R, oi_R, qdd_R, gci_R], dma=True, n_dma=5)
        kb.stage("vector", lambda v: v.memset(NB2.ap()[0:2, :], 1.0),
                 writes=[nb2_R])
        kb.stage("vector", lambda v: v.memset(mg_v, RSQRT_MAGIC),
                 writes=[mg_R])

        kb.stage("gpsimd", lambda h: h.dma_start(
            out=CB2.ap()[1:2, :], in_=bia_in.ap()[:, :]),
            writes=[cb2_R], dma=True)
        kb.stage("vector", lambda v: v.memset(CB2.ap()[0:1, 6144:7168], 0.0),
                 reads=[], writes=[cb2_R])
        for kc in range(8):
            kb.stage("gpsimd", lambda h, kc=kc: h.dma_start(
                out=wqkv_t[kc], in_=wqkvp.ap()[kc * 128:(kc + 1) * 128, :]),
                writes=[wqkv_R], dma=True)
        for kc in range(8):
            kb.stage("gpsimd", lambda h, kc=kc: h.dma_start(
                out=wg_t[kc], in_=wg_in.ap()[kc * 128:(kc + 1) * 128, :]),
                writes=[wg_R], dma=True)
        for nt in range(8):
            def mk_cs(h, nt=nt):
                last = None
                for kc in range(8):
                    last = h.matmul(ps0.ap()[0:1, 0:512], ones_col,
                                    wqkv_t[kc][:, nt * 512:(nt + 1) * 512],
                                    start=(kc == 0), stop=(kc == 7))
                return last
            kb.stage("tensor", mk_cs, reads=[wqkv_R, oi_R], writes=[ps0_R])
            kb.stage("scalar", lambda s, nt=nt: s.copy(
                CB2.ap()[0:1, nt * 512:(nt + 1) * 512], ps0.ap()[0:1, 0:512]),
                reads=[ps0_R], writes=[cb2_R])
        for nt in range(4):
            def mk_csg(h, nt=nt):
                last = None
                for kc in range(8):
                    last = h.matmul(ps1.ap()[0:1, 0:512], ones_col,
                                    wg_t[kc][:, nt * 512:(nt + 1) * 512],
                                    start=(kc == 0), stop=(kc == 7))
                return last
            kb.stage("tensor", mk_csg, reads=[wg_R, oi_R], writes=[ps1_R])
            kb.stage("scalar", lambda s, nt=nt: s.copy(
                CB2.ap()[0:1, 4096 + nt * 512:4096 + (nt + 1) * 512],
                ps1.ap()[0:1, 0:512]),
                reads=[ps1_R], writes=[cb2_R])

        def shard_tile(w_dram, row0, prow, col0, ncols, flat_base, stride_i,
                       shard_col_off):
            kb.stage("sync", lambda h: h.dma_start(
                out=W0.ap()[0:prow, 0:ncols],
                in_=w_dram.ap()[row0:row0 + prow, col0:col0 + ncols]),
                writes=[w0_R], dma=True)
            na = min(512, ncols)
            calls = [(prow, na, W0.ap()[0:prow, 0:na], w0_R,
                      W0Q.ap()[0:prow, 0:na], w0q_R, qsA)]
            if ncols > na:
                calls.append((prow, ncols - na,
                              W0.ap()[0:prow, na:ncols], w0_R,
                              W0Q.ap()[0:prow, na:ncols], w0q_R, qsB))
            emit_quant_multi(kb, calls)
            ntile = ncols // 128
            for base in range(0, ntile, 8):
                nn = min(8, ntile - base)

                def tr(h, base=base, nn=nn, prow=prow):
                    last = None
                    for j in range(nn):
                        last = h.transpose(
                            pst.ap()[:, j * 128:j * 128 + prow],
                            W0Q.ap()[0:prow,
                                     (base + j) * 128:(base + j + 1) * 128],
                            ident[0:prow, 0:prow])
                    return last
                kb.stage("tensor", tr, reads=[w0q_R, oi_R], writes=[pst_R[0]])

                def ev(s, nn=nn, prow=prow):
                    return s.copy(
                        W0S.ap()[:, 0:nn * prow]
                        .rearrange("p (j q) -> p j q", q=prow),
                        pst.ap()[:, 0:nn * 128]
                        .rearrange("p (j q) -> p j q", q=128)[:, :, 0:prow])
                kb.stage("scalar", ev, reads=[pst_R[0]], writes=[w0s_R])

                def out_dma(h, base=base, nn=nn, prow=prow):
                    dest = _dram_view(
                        wsh_in,
                        flat_base + (col0 + base * 128) * stride_i + shard_col_off,
                        [(stride_i, 128), (128 * stride_i, nn), (1, prow)])
                    return h.dma_start(
                        out=dest,
                        in_=W0S.ap()[:, 0:nn * prow]
                        .rearrange("p (j q) -> p j q", q=prow))
                kb.stage("sync", out_dma, reads=[w0s_R], writes=[wsh_R],
                         dma=True)


        shard_thunks = []
        if not lite0:
            for off0 in range(0, SH_GU, 128):
                prow = min(128, SH_GU - off0)
                shard_thunks.append(
                    lambda off0=off0, prow=prow: shard_tile(
                        wgate, off0, prow, 0, 1024, 0, SH_GU, off0))
                shard_thunks.append(
                    lambda off0=off0, prow=prow: shard_tile(
                        wup, off0, prow, 0, 1024, WU_OFF, SH_GU, off0))
            for (c0, ncols) in [(0, 1024), (1024, 1024), (2048, 768)]:
                shard_thunks.append(
                    lambda c0=c0, ncols=ncols: shard_tile(
                        wdown, 0, 128, c0, ncols, WD_OFF, SH_D, 0))
            shard_thunks.append(lambda: kb.stage(
                "gpsimd", lambda g: g.collective_compute(
                    "AllGather", ALU.bypass,
                    replica_groups=[list(range(NCORE))],
                    ins=[wsh_in.ap().opt()], outs=[wsh_all.ap().opt()]),
                reads=[wsh_R], writes=[wshall_R], cc=True))


        if stop_after < 1:
            kb.barrier()
            kb.finalize()
            return nc, dbg_outs
        kb.stage("vector", lambda v: v.memset(P_v, 0.0), writes=P_R)
        retscr_R = [R(f"retd{c}") for c in range(NCH)]
        qdscr_R = [R(f"qdd{c}") for c in range(NCH)]
        xfscr_R = [R(f"xfd{c}") for c in range(NCH)]
        x2qscr_R = [R(f"x2qd{c}") for c in range(NCH)]
        x2scr_R = [R(f"x2d{c}") for c in range(NCH)]
        sgscr_R = [R(f"sgd{c}") for c in range(NCH)]

        tmp1, tmp1R = ssv["tmp1"]
        tmp2, tmp2R = ssv["tmp2"]
        stm, stmR = ssv["stm"]
        stv, stvR = ssv["stv"]
        smq, smqR = ssv["smsq"]

        def hv(ap, half):
            return ap.rearrange("p (g s) -> p g s", s=128)[:, :, half * 64:(half + 1) * 64]

        def t3(ap):
            return ap.rearrange("p (g s) -> p g s", s=64)

        def cs_view(pc, part):
            return cs_v[pc][:, part * 512:(part + 1) * 512] \
                .rearrange("p (g s) -> p g s", s=64)

        def p1A_ld(c):
            pc = c % 2
            kb.stage("gpsimd", lambda h, c=c, pc=pc: h.dma_start(
                out=xb_v[pc], in_=xsl.ap()[c * CH:(c + 1) * CH, :]),
                writes=[xb_R[pc]], dma=True)

        def p1A(c):
            pc = c % 2
            kb.stage("sync", lambda h, c=c, pc=pc: h.dma_start(
                out=cs_v[pc], in_=cs_in.ap()[c * CH:(c + 1) * CH, :]),
                writes=[cs_R[pc]], dma=True)
            st = stm
            kb.stage("scalar", lambda a, pc=pc: a.activation(
                out=xfm_v, in_=xb_v[pc], func=AF.Square,
                accum_out=st[:, 0:1]),
                reads=[xb_R[pc]], writes=[xfm_R, stmR])
            kb.stage("vector", lambda v, pc=pc: v.reduce_sum(
                st[:, 1:2], xb_v[pc], AXX),
                reads=[xb_R[pc], stmR], writes=[stmR])
            kb.stage("vector", lambda v: v.tensor_scalar(
                st[:, 2:3], st[:, 1:2], 1.0 / 1024.0, None, ALU.mult),
                reads=[stmR], writes=[stmR])
            kb.stage("vector", lambda v: v.tensor_scalar(
                st[:, 3:4], st[:, 0:1], 1.0 / 1024.0, LN_EPS,
                ALU.mult, ALU.add),
                reads=[stmR], writes=[stmR])
            kb.stage("vector", lambda v: v.tensor_tensor(
                st[:, 4:5], st[:, 2:3], st[:, 2:3], ALU.mult),
                reads=[stmR], writes=[stmR])
            kb.stage("vector", lambda v: v.tensor_tensor(
                st[:, 3:4], st[:, 3:4], st[:, 4:5], ALU.subtract),
                reads=[stmR], writes=[stmR])
            emit_rsqrt(kb, st[:, 3:4], stmR, st[:, 8:9], stmR,
                       st[:, 6:7], stmR, st[:, 7:8], stmR, mg_v, mg_R, 1)
            kb.stage("vector", lambda v: v.scalar_tensor_tensor(
                st[:, 9:10], st[:, 2:3], -1.0, st[:, 8:9],
                ALU.mult, ALU.mult),
                reads=[stmR], writes=[stmR])
            kb.stage("vector", lambda v: v.tensor_copy(
                rstd_v[:, 0:1], st[:, 9:10]),
                reads=[stmR], writes=[rstd_R])
            kb.stage("tensor", lambda h: h.transpose(
                pst.ap()[0:1, 1024:1152], rstd_v[:, 0:1], ident),
                reads=[rstd_R, oi_R], writes=[pst_R[1]])
            kb.stage("scalar", lambda a, c=c: a.copy(
                NB2.ap()[0:1, c * 128:(c + 1) * 128],
                pst.ap()[0:1, 1024:1152]),
                reads=[pst_R[1]], writes=[nb2_R])
            kb.stage("vector", lambda v, pc=pc: v.tensor_scalar(
                xsq_v, xb_v[pc], st[:, 8:9], None, ALU.mult),
                reads=[xb_R[pc], stmR], writes=[xsq_R])

        def p1A2(c):
            pc = c % 2

            def trx(h, pc=pc):
                last = None
                for j in range(8):
                    last = h.transpose(pst.ap()[:, j * 128:(j + 1) * 128],
                                       xsq_v[:, j * 128:(j + 1) * 128],
                                       ident)
                return last
            kb.stage("tensor", trx, reads=[xsq_R, oi_R], writes=[pst_R[0]])
            kb.stage("scalar", lambda s, pc=pc: s.copy(
                xfc_v[pc], pst.ap()[:, 0:1024]),
                reads=[pst_R[0]], writes=[xfc_R[pc]])
            kb.stage("sync", lambda h, c=c, pc=pc: h.dma_start(
                out=xfms_scr.ap()[c * CH:(c + 1) * CH, :], in_=xfc_v[pc]),
                reads=[xfc_R[pc]], writes=[xfscr_R[c]], dma=True)
        def p1B(c):
            pc = c % 2

            def qkv_nt(h, nt, out_ap, c, pc, w_t, cbase):
                last = None
                for kc in range(8):
                    last = h.matmul(out_ap,
                                    xfc_v[pc][:, kc * 128:(kc + 1) * 128],
                                    w_t[kc][:, nt * 512:(nt + 1) * 512],
                                    start=(kc == 0), stop=False)
                last = h.matmul(out_ap, NB2.ap()[0:2, c * 128:(c + 1) * 128],
                                CB2.ap()[0:2, cbase + nt * 512:cbase + (nt + 1) * 512],
                                start=False, stop=True)
                return last

            kb.stage("tensor", lambda h, c=c, pc=pc: (
                qkv_nt(h, 0, psq.ap()[:, 0:512], c, pc, wqkv_t, 0),
                qkv_nt(h, 1, psq.ap()[:, 512:1024], c, pc, wqkv_t, 0))[-1],
                reads=[xfc_R[pc], wqkv_R, nb2_R, cb2_R],
                writes=psq_R)
            kb.stage("scalar", lambda s: s.copy(qsb_v, psq.ap()[:, :]),
                     reads=psq_R, writes=[qsb_R])
            kb.stage("tensor", lambda h, c=c, pc=pc: (
                qkv_nt(h, 2, psk.ap()[:, 0:512], c, pc, wqkv_t, 0),
                qkv_nt(h, 3, psk.ap()[:, 512:1024], c, pc, wqkv_t, 0))[-1],
                reads=[xfc_R[pc], wqkv_R, nb2_R, cb2_R],
                writes=psk_R)
            kb.stage("scalar", lambda s: s.copy(ksb_v, psk.ap()[:, :]),
                     reads=psk_R, writes=[ksb_R])
            for i, nt in enumerate(range(4, 8)):
                pv = [ps0, ps1][i % 2]
                pvR = [ps0_R, ps1_R][i % 2]
                kb.stage("tensor", lambda h, nt=nt, pv=pv, c=c, pc=pc:
                         qkv_nt(h, nt, pv.ap()[:, 0:512], c, pc, wqkv_t, 0),
                         reads=[xfc_R[pc], wqkv_R, nb2_R, cb2_R],
                         writes=[pvR])
                kb.stage("scalar", lambda s, i=i, pv=pv: s.copy(
                    vc_v[:, i * 512:(i + 1) * 512], pv.ap()[:, 0:512]),
                    reads=[pvR], writes=[vc_R])

            for (src_v, src_R, dst_v, dst_R, cpart, spart) in [
                    (qsb_v, qsb_R, qpr_v, qpr_R, 0, 1),
                    (ksb_v, ksb_R, kdc_v, kdc_R, 2, 3)]:
                kb.stage("gpsimd", lambda g, src_v=src_v, src_R=src_R, pc=pc,
                         cpart=cpart:
                         g.tensor_tensor(t3(tmp1), hv(src_v, 0),
                                         cs_view(pc, cpart), ALU.mult),
                         reads=[src_R, cs_R[pc]], writes=[tmp1R])
                kb.stage("vector", lambda g, src_v=src_v, src_R=src_R, pc=pc,
                         spart=spart:
                         g.tensor_tensor(t3(tmp2), hv(src_v, 1),
                                         cs_view(pc, spart), ALU.mult),
                         reads=[src_R, cs_R[pc]], writes=[tmp2R])
                kb.stage("vector", lambda v, dst_v=dst_v, dst_R=dst_R:
                         v.tensor_tensor(hv(dst_v, 0), t3(tmp1), t3(tmp2),
                                         ALU.subtract),
                         reads=[tmp1R, tmp2R], writes=[dst_R])
                kb.stage("gpsimd", lambda g, src_v=src_v, src_R=src_R, pc=pc,
                         cpart=cpart:
                         g.tensor_tensor(t3(tmp1), hv(src_v, 1),
                                         cs_view(pc, cpart), ALU.mult),
                         reads=[src_R, cs_R[pc]], writes=[tmp1R])
                kb.stage("vector", lambda g, src_v=src_v, src_R=src_R, pc=pc,
                         spart=spart:
                         g.tensor_tensor(t3(tmp2), hv(src_v, 0),
                                         cs_view(pc, spart), ALU.mult),
                         reads=[src_R, cs_R[pc]], writes=[tmp2R])
                kb.stage("vector", lambda v, dst_v=dst_v, dst_R=dst_R:
                         v.tensor_tensor(hv(dst_v, 1), t3(tmp1), t3(tmp2),
                                         ALU.add),
                         reads=[tmp1R, tmp2R], writes=[dst_R])

            kb.stage("vector", lambda v: v.tensor_tensor(
                qdt_v, qpr_v, QDD.ap()[:, :], ALU.mult),
                reads=[qpr_R, qdd_R], writes=[qdt_R])
        def p1C(c):
            pc = c % 2

            for (src_v, src_R, dst_v, dst_R, half, alias) in [
                    (qpr_v, qpr_R, qfm_v, qfm_R, 0, None),
                    (qdt_v, qdt_R, qdstg_v, qdstg_R, 1, ksb_R),
                    (kdc_v, kdc_R, kfm_v, kfm_R, 0, None)]:
                def trf(h, src_v=src_v, half=half):
                    last = None
                    for j in range(8):
                        last = h.transpose(
                            pst.ap()[:, half * 1024 + j * 128:
                                     half * 1024 + (j + 1) * 128],
                            src_v[:, j * 128:(j + 1) * 128], ident)
                    return last
                kb.stage("tensor", trf, reads=[src_R, oi_R],
                         writes=[pst_R[half]])
                wl = [dst_R] if alias is None else [dst_R, alias]
                kb.stage("scalar", lambda s, dst_v=dst_v, half=half: s.copy(
                    dst_v, pst.ap()[:, half * 1024:(half + 1) * 1024]),
                    reads=[pst_R[half]], writes=wl)
            kb.stage("sync", lambda h, c=c: h.dma_start(
                out=qdt_scr.ap()[c * CH:(c + 1) * CH, :], in_=qdstg_v),
                reads=[qdstg_R], writes=[qdscr_R[c]], dma=True)

            def sc_mm(h):
                last = None
                for h8 in range(H):
                    last = h.matmul(psq.ap()[:, h8 * 128:(h8 + 1) * 128],
                                    kfm_v[:, h8 * 128:(h8 + 1) * 128],
                                    qfm_v[:, h8 * 128:(h8 + 1) * 128],
                                    start=True, stop=True)
                return last
            kb.stage("tensor", sc_mm, reads=[kfm_R, qfm_R], writes=psq_R)
            for i in range(2):
                kb.stage("vector", lambda v, i=i: v.tensor_tensor(
                    scr_v[:, i * 512:(i + 1) * 512],
                    psq.ap()[:, i * 512:(i + 1) * 512],
                    DTP.ap()[:, i * 512:(i + 1) * 512], ALU.mult),
                    reads=[psq_R[i], dtp_R],
                    writes=[scr_R, qsb_R] if i == 0 else [scr_R])

            for r in range(4):
                pv = [ps0, ps1][r % 2]
                pvR = [ps0_R, ps1_R][r % 2]
                hh = [2 * r, 2 * r + 1]

                def ret_mm(h, hh=hh, pv=pv):
                    last = None
                    for i, h8 in enumerate(hh):
                        cols = pv.ap()[:, i * 256:(i + 1) * 256]
                        last = h.matmul(cols,
                                        scr_v[:, h8 * 128:(h8 + 1) * 128],
                                        vc_v[:, h8 * 256:(h8 + 1) * 256],
                                        start=True, stop=False)
                        last = h.matmul(cols,
                                        qdstg_v[:, h8 * 128:(h8 + 1) * 128],
                                        P_v[:, h8 * 256:(h8 + 1) * 256],
                                        start=False, stop=True)
                    return last
                kb.stage("tensor", ret_mm,
                         reads=[scr_R, vc_R, qdstg_R, P_R[r // 2]],
                         writes=[pvR])
                kb.stage("scalar", lambda s, r=r, pv=pv: s.copy(
                    ret_v[:, r * 512:(r + 1) * 512], pv.ap()[:, 0:512]),
                    reads=[pvR], writes=[ret_R])
            kb.stage("sync", lambda h, c=c: h.dma_start(
                out=ret_scr.ap()[c * CH:(c + 1) * CH, :], in_=ret_v),
                reads=[ret_R], writes=[retscr_R[c]], dma=True)

            for sr in range(2):
                hh = [4 * sr + i for i in range(4)]

                def st_mm(h, hh=hh):
                    last = None
                    for i, h8 in enumerate(hh):
                        cols = psk.ap()[:, i * 256:(i + 1) * 256]
                        last = h.matmul(cols,
                                        GCI.ap()[:, h8 * 128:(h8 + 1) * 128],
                                        P_v[:, h8 * 256:(h8 + 1) * 256],
                                        start=True, stop=False)
                        last = h.matmul(cols,
                                        kdc_v[:, h8 * 128:(h8 + 1) * 128],
                                        vc_v[:, h8 * 256:(h8 + 1) * 256],
                                        start=False, stop=True)
                    return last
                kb.stage("tensor", st_mm,
                         reads=[gci_R, kdc_R, vc_R, P_R[sr]],
                         writes=psk_R)
                kb.stage("scalar", lambda s, sr=sr: s.copy(
                    P_half[sr], psk.ap()[:, :]),
                    reads=psk_R, writes=[P_R[sr]])

        NC1 = min(NCH, nch1)
        p1A_ld(0)
        for t in range(NC1 + 3):
            if t + 1 < NC1:
                p1A_ld(t + 1)
            if 0 <= t - 1 < NC1:
                p1A2(t - 1)
            if 0 <= t - 3 < NC1:
                p1C(t - 3)
            if 0 <= t - 2 < NC1:
                p1B(t - 2)
            if t < NC1:
                p1A(t)
            if 0 <= t - 3 < NC1 and shard_thunks:
                shard_thunks.pop(0)()

        while shard_thunks:
            shard_thunks.pop(0)()


        if stop_after < 1.5:
            kb.barrier()
            kb.finalize()
            return nc, dbg_outs
        kb.stage("gpsimd", lambda h: h.dma_start(out=st_in.ap()[:, :], in_=P_v),
                 reads=P_R, writes=[stin_R], dma=True)
        kb.stage("gpsimd", lambda g: g.collective_compute(
            "AllGather", ALU.bypass, replica_groups=[list(range(NCORE))],
            ins=[st_in.ap().opt()], outs=[st_all.ap().opt()]),
            reads=[stin_R], writes=[stall_R], cc=True)
        kb.stage("vector", lambda v: v.memset(sext_v, 0.0), writes=[sext_R])

        def seg0(c):
            pc = c % 2
            kb.stage("sync", lambda h, c=c, pc=pc: h.dma_start(
                out=xfc_v[pc], in_=xfms_scr.ap()[c * CH:(c + 1) * CH, :]),
                reads=[xfscr_R[c]], writes=[xfc_R[pc]], dma=True)
            for nt in range(4):
                pv = [psq, psk][nt % 2]
                pvR = [psq_R[0], psk_R[0], psq_R[1], psk_R[1]][nt]
                hoff = ((nt // 2) % 2) * 512

                def g_mm(h, nt=nt, pv=pv, hoff=hoff, c=c, pc=pc):
                    out = pv.ap()[:, hoff:hoff + 512]
                    last = None
                    for kc in range(8):
                        last = h.matmul(out,
                                        xfc_v[pc][:, kc * 128:(kc + 1) * 128],
                                        wg_t[kc][:, nt * 512:(nt + 1) * 512],
                                        start=(kc == 0), stop=False)
                    last = h.matmul(out, NB2.ap()[0:2, c * 128:(c + 1) * 128],
                                    CB2.ap()[0:2, 4096 + nt * 512:
                                             4096 + (nt + 1) * 512],
                                    start=False, stop=True)
                    return last
                kb.stage("tensor", g_mm,
                         reads=[xfc_R[pc], wg_R, nb2_R, cb2_R],
                         writes=[pvR])
                kb.stage("scalar", lambda s, nt=nt, pv=pv, hoff=hoff:
                         s.activation(
                             out=sgstg_v[:, nt * 512:(nt + 1) * 512],
                             in_=pv.ap()[:, hoff:hoff + 512], func=AF.Silu),
                         reads=[pvR], writes=[sgstg_R])
            kb.stage("sync", lambda h, c=c: h.dma_start(
                out=sg_scr.ap()[c * CH:(c + 1) * CH, :], in_=sgstg_v),
                reads=[sgstg_R], writes=[sgscr_R[c]], dma=True)

        for r in range(NCORE):
            kb.stage("sync", lambda h, r=r: h.dma_start(
                out=sgtmp_v, in_=st_all.ap()[r * 128:(r + 1) * 128, :]),
                reads=[stall_R, vc_R], writes=[sgtmp_R], dma=True)

            def comb(v, r=r):
                last = None
                for h8 in range(H):
                    last = v.scalar_tensor_tensor(
                        sext_v[:, h8 * 256:(h8 + 1) * 256],
                        sgtmp_v[:, h8 * 256:(h8 + 1) * 256],
                        COEF.ap()[:, r * 8 + h8:r * 8 + h8 + 1],
                        sext_v[:, h8 * 256:(h8 + 1) * 256],
                        ALU.mult, ALU.add)
                return last
            kb.stage("vector", comb, reads=[sgtmp_R, con_R, sext_R],
                     writes=[sext_R])

        kb.barrier()
        donors = [vc_R, P_R[0], P_R[1], ret_R, dtp_R, qdd_R, gci_R, w0_R,
                  w0q_R, w0s_R, qdt_R, sgtmp_R]
        for kc in range(16):
            kb.stage("gpsimd", lambda h, kc=kc: h.dma_start(
                out=wp_t[kc], in_=wp_in.ap()[kc * 128:(kc + 1) * 128, :]),
                writes=([wp_R] + donors) if kc == 0 else [wp_R], dma=True)

        if dbg:
            dR = R("dbgc")
            kb.stage("vector", lambda v: v.tensor_copy(g_v, sext_v),
                     reads=[sext_R], writes=[dR])
            kb.stage("sync", lambda h: h.dma_start(
                out=dbg_outs["d_sext"].ap()[:, :], in_=g_v),
                reads=[dR], dma=True)
            dR2 = R("dbgc2")
            kb.stage("vector", lambda v: v.tensor_copy(retp_v[0], P_v),
                     reads=P_R, writes=[dR2])
            kb.stage("sync", lambda h: h.dma_start(
                out=dbg_outs["d_P"].ap()[:, :], in_=retp_v[0]),
                reads=[dR2], dma=True)

        if stop_after < 2:
            kb.barrier()
            kb.finalize()
            return nc, dbg_outs

        gp = [g_v, f32v(A5, 0, 2048)]
        gpR = [g_R, R("g1")]
        gnp = [ssv["gn"][0], ssv["tmp2"][0][:, 0:64]]
        gnpR = [ssv["gn"][1], R("gn1")]
        ln2p = [ssv["ln2"][0], ssv["tmp2"][0][:, 64:128]]
        ln2pR = [ssv["ln2"][1], R("ln21")]
        normp = [norm_v, bfv(A5, 4096, 2048)]
        normpR = [norm_R, R("norm1")]
        gatedp = [gated_v, bfv(A5, 8192, 2048)]
        gatedpR = [gated_R, R("gated1")]
        x2p = [x2_v, f32v(T1, 18432, 1024)]
        x2pR = [x2_R, R("x21")]
        x2qp = [x2q_v, bfv(A5, 10240, 1024)]
        x2qpR = [x2q_R, R("x2q1")]
        qdinp = [qdin_v, bfv(A5, 11264, 1024)]
        qdinpR = [qdin_R, R("qdin1")]
        gsqd = [ssv["tmp1"][0][:, 0:256], ssv["tmp1"][0][:, 256:512]]
        gsqdR = [ssv["tmp1"][1], R("gsqd1")]

        def seg1(c):
            pc = c % 2
            pc = c % 2
            kb.stage("sync", lambda h, c=c, pc=pc: [
                h.dma_start(out=retp_v[pc],
                            in_=ret_scr.ap()[c * CH:(c + 1) * CH, :]),
                h.dma_start(out=xc_v[pc],
                            in_=xsl.ap()[c * CH:(c + 1) * CH, :]),
                h.dma_start(out=sgin_v[pc],
                            in_=sg_scr.ap()[c * CH:(c + 1) * CH, :]),
                h.dma_start(out=qdinp[pc],
                            in_=qdt_scr.ap()[c * CH:(c + 1) * CH, :]),
            ], reads=[retscr_R[c], sgscr_R[c], qdscr_R[c]],
                writes=[retp_R[pc], xc_R[pc], sgin_R[pc], qdinpR[pc]], dma=True,
                n_dma=4)
            if c > 0:
                def sdecay(g):
                    last = None
                    for h8 in range(H):
                        last = g.tensor_scalar(
                            sext_v[:, h8 * 256:(h8 + 1) * 256],
                            sext_v[:, h8 * 256:(h8 + 1) * 256],
                            float(GAM_PY[h8] ** CH), None, ALU.mult)
                    return last
                kb.stage("vector", sdecay, reads=[sext_R], writes=[sext_R])
            for r in range(4):
                pv = [ps0, ps1][r % 2]
                pvR = [ps0_R, ps1_R][r % 2]
                hh = [2 * r, 2 * r + 1]

                def cr_mm(h, hh=hh, pv=pv, pc=pc):
                    last = None
                    for i, h8 in enumerate(hh):
                        last = h.matmul(pv.ap()[:, i * 256:(i + 1) * 256],
                                        qdinp[pc][:, h8 * 128:(h8 + 1) * 128],
                                        sext_v[:, h8 * 256:(h8 + 1) * 256],
                                        start=True, stop=True)
                    return last
                kb.stage("tensor", cr_mm, reads=[qdinpR[pc], sext_R], writes=[pvR])

                def gasm(v, hh=hh, pv=pv, pc=pc):
                    last = None
                    for i, h8 in enumerate(hh):
                        last = v.scalar_tensor_tensor(
                            gp[pc][:, h8 * 256:(h8 + 1) * 256],
                            pv.ap()[:, i * 256:(i + 1) * 256], 1.0,
                            retp_v[pc][:, h8 * 256:(h8 + 1) * 256],
                            ALU.mult, ALU.add,
                            accum_out=gnp[pc][:, h8:h8 + 1])
                    return last
                kb.stage("vector", gasm, reads=[pvR, retp_R[pc], gnpR[pc]],
                         writes=[gpR[pc], gnpR[pc]])
        def seg2h(c):
            pc = c % 2
            for h8 in range(H):
                kb.stage("scalar", lambda a, h8=h8, pc=pc: a.activation(
                    out=gsqd[pc][:, 0:256],
                    in_=gp[pc][:, h8 * 256:(h8 + 1) * 256],
                    func=AF.Square, accum_out=gnp[pc][:, 32 + h8:33 + h8]),
                    reads=[gpR[pc]], writes=[gsqdR[pc], gnpR[pc]])

        def seg2v(c, q):
            pc = c % 2
            q.stage("vector", lambda v, pc=pc: v.tensor_scalar(
                gnp[pc][:, 8:16], gnp[pc][:, 0:8], 1.0 / 256.0, None, ALU.mult),
                reads=[gnpR[pc]], writes=[gnpR[pc]])
            q.stage("vector", lambda v, pc=pc: v.tensor_scalar(
                gnp[pc][:, 40:48], gnp[pc][:, 32:40], 1.0 / 256.0, GN_EPS,
                ALU.mult, ALU.add),
                reads=[gnpR[pc]], writes=[gnpR[pc]])
            q.stage("vector", lambda v, pc=pc: v.tensor_tensor(
                gnp[pc][:, 16:24], gnp[pc][:, 8:16], gnp[pc][:, 8:16], ALU.mult),
                reads=[gnpR[pc]], writes=[gnpR[pc]])
            q.stage("vector", lambda v, pc=pc: v.tensor_tensor(
                gnp[pc][:, 40:48], gnp[pc][:, 40:48], gnp[pc][:, 16:24], ALU.subtract),
                reads=[gnpR[pc]], writes=[gnpR[pc]])
            emit_rsqrt(q, gnp[pc][:, 40:48], gnpR[pc], gnp[pc][:, 40:48], gnpR[pc],
                       gnp[pc][:, 48:56], gnpR[pc], gnp[pc][:, 56:64], gnpR[pc], mg_v, mg_R, 8)
            q.stage("vector", lambda v, pc=pc: v.scalar_tensor_tensor(
                gnp[pc][:, 24:32], gnp[pc][:, 8:16], -1.0, gnp[pc][:, 40:48],
                ALU.mult, ALU.mult),
                reads=[gnpR[pc]], writes=[gnpR[pc]])
            for h8 in range(H):
                q.stage("vector", lambda v, pc=pc, h8=h8: v.tensor_scalar(
                    normp[pc][:, h8 * 256:(h8 + 1) * 256],
                    gp[pc][:, h8 * 256:(h8 + 1) * 256],
                    gnp[pc][:, 40 + h8:41 + h8],
                    gnp[pc][:, 24 + h8:25 + h8], ALU.mult, ALU.add),
                    reads=[gpR[pc], gnpR[pc]], writes=[normpR[pc]])
            q.stage("vector", lambda v, pc=pc: v.tensor_tensor(
                gatedp[pc], normp[pc], sgin_v[pc], ALU.mult),
                reads=[normpR[pc], sgin_R[pc]], writes=[gatedpR[pc]])
        def seg3a(c):
            pc = c % 2
            for half in range(2):
                def trg(h, half=half, pc=pc):
                    last = None
                    for j in range(8):
                        last = h.transpose(
                            pst.ap()[:, half * 1024 + j * 128:
                                     half * 1024 + (j + 1) * 128],
                            gatedp[pc][:, half * 1024 + j * 128:
                                    half * 1024 + (j + 1) * 128],
                            ident)
                    return last
                kb.stage("tensor", trg, reads=[gatedpR[pc], oi_R],
                         writes=[pst_R[half]])
                kb.stage("scalar", lambda s, half=half: s.copy(
                    gatT_v[:, half * 1024:(half + 1) * 1024],
                    pst.ap()[:, half * 1024:(half + 1) * 1024]),
                    reads=[pst_R[half]], writes=[gatT_R])

        def seg3b(c):
            pc = c % 2
            for nt in range(2):
                def p_mm(h, nt=nt, c=c):
                    out = psq.ap()[:, nt * 512:(nt + 1) * 512]
                    last = None
                    for jv in range(16):
                        last = h.matmul(out,
                                        gatT_v[:, jv * 128:(jv + 1) * 128],
                                        wp_t[jv][:, nt * 512:(nt + 1) * 512],
                                        start=(jv == 0), stop=False)
                    last = h.matmul(out, NB2.ap()[0:2, c * 128:(c + 1) * 128],
                                    CB2.ap()[0:2, 6144 + nt * 512:
                                             6144 + (nt + 1) * 512],
                                    start=False, stop=True)
                    return last
                kb.stage("tensor", p_mm, reads=[gatT_R, wp_R, nb2_R, cb2_R],
                         writes=[psq_R[nt]])
                kb.stage("vector", lambda v, nt=nt, pc=pc:
                         v.scalar_tensor_tensor(
                             x2p[pc][:, nt * 512:(nt + 1) * 512],
                             psq.ap()[:, nt * 512:(nt + 1) * 512], 1.0,
                             xc_v[pc][:, nt * 512:(nt + 1) * 512],
                             ALU.mult, ALU.add,
                             accum_out=ln2p[pc][:, nt:nt + 1]),
                         reads=[psq_R[nt], xc_R[pc], ln2pR[pc]],
                         writes=[x2pR[pc], ln2pR[pc]])
            kb.stage("sync", lambda h, c=c, pc=pc: h.dma_start(
                out=x2_scr.ap()[c * CH:(c + 1) * CH, :], in_=x2p[pc]),
                reads=[x2pR[pc]], writes=[x2scr_R[c]], dma=True)
            kb.stage("scalar", lambda a, pc=pc: a.activation(
                out=qsap["ax"][:, 0:1024], in_=x2p[pc], func=AF.Square,
                accum_out=ln2p[pc][:, 5:6]),
                reads=[x2pR[pc]],
                writes=[qsA["ax"][1], qsB["ax"][1], ln2pR[pc]])

        def seg3v(c, q):
            pc = c % 2
            q.stage("vector", lambda v, pc=pc: v.tensor_tensor(
                ln2p[pc][:, 4:5], ln2p[pc][:, 0:1], ln2p[pc][:, 1:2], ALU.add),
                reads=[ln2pR[pc]], writes=[ln2pR[pc]])
            q.stage("vector", lambda v, pc=pc: v.tensor_scalar(
                ln2p[pc][:, 6:7], ln2p[pc][:, 4:5], 1.0 / 1024.0, None, ALU.mult),
                reads=[ln2pR[pc]], writes=[ln2pR[pc]])
            q.stage("vector", lambda v, pc=pc: v.tensor_scalar(
                ln2p[pc][:, 7:8], ln2p[pc][:, 5:6], 1.0 / 1024.0, LN_EPS,
                ALU.mult, ALU.add),
                reads=[ln2pR[pc]], writes=[ln2pR[pc]])
            q.stage("vector", lambda v, pc=pc: v.tensor_tensor(
                ln2p[pc][:, 8:9], ln2p[pc][:, 6:7], ln2p[pc][:, 6:7], ALU.mult),
                reads=[ln2pR[pc]], writes=[ln2pR[pc]])
            q.stage("vector", lambda v, pc=pc: v.tensor_tensor(
                ln2p[pc][:, 7:8], ln2p[pc][:, 7:8], ln2p[pc][:, 8:9], ALU.subtract),
                reads=[ln2pR[pc]], writes=[ln2pR[pc]])
            emit_rsqrt(q, ln2p[pc][:, 7:8], ln2pR[pc], ln2p[pc][:, 7:8], ln2pR[pc],
                       ln2p[pc][:, 10:11], ln2pR[pc], ln2p[pc][:, 11:12], ln2pR[pc], mg_v, mg_R, 1)
            q.stage("vector", lambda v, pc=pc: v.scalar_tensor_tensor(
                ln2p[pc][:, 9:10], ln2p[pc][:, 6:7], -1.0, ln2p[pc][:, 7:8],
                ALU.mult, ALU.mult),
                reads=[ln2pR[pc]], writes=[ln2pR[pc]])
            q.stage("vector", lambda v, pc=pc: v.tensor_scalar(
                x2p[pc], x2p[pc], ln2p[pc][:, 7:8], ln2p[pc][:, 9:10], ALU.mult, ALU.add),
                reads=[x2pR[pc], ln2pR[pc]], writes=[x2pR[pc]])
        def seg4a(c):
            pc = c % 2
            emit_quant_multi(kb, [
                (128, 512, x2p[pc][:, 0:512], x2pR[pc],
                 x2qp[pc][:, 0:512], x2qpR[pc], qsA, ("q",), False, True),
                (128, 512, x2p[pc][:, 512:1024], x2pR[pc],
                 x2qp[pc][:, 512:1024], x2qpR[pc], qsB, ("q",), False,
                 True)])

        def seg4b(c):
            pc = c % 2

            def trq(h, pc=pc):
                last = None
                for j in range(8):
                    last = h.transpose(pst.ap()[:, j * 128:(j + 1) * 128],
                                       x2qp[pc][:, j * 128:(j + 1) * 128], ident)
                return last
            kb.stage("tensor", trq, reads=[x2qpR[pc], oi_R], writes=[pst_R[0]])
            kb.stage("scalar", lambda s, pc=pc: s.copy(gatedp[pc][:, 0:1024],
                                                pst.ap()[:, 0:1024]),
                     reads=[pst_R[0]], writes=[gatedpR[pc]])
            kb.stage("sync", lambda h, c=c, pc=pc: h.dma_start(
                out=x2qt_scr.ap()[c * CH:(c + 1) * CH, :],
                in_=gatedp[pc][:, 0:1024]),
                reads=[gatedpR[pc]], writes=[x2qscr_R[c]], dma=True)


        NC2 = min(NCH, nch1)
        for t in range(NC2 + 7):
            c5, c4, c3, c2, c1, c0 = t - 6, t - 5, t - 4, t - 3, t - 2, t
            if 0 <= c4 < NC2:
                seg4a(c4)
            if 0 <= c3 < NC2:
                seg3a(c3)
                seg3b(c3)
            if 0 <= c5 < NC2:
                seg4b(c5)
            if 0 <= c2 < NC2:
                seg2h(c2)
            if 0 <= c3 < NC2:
                seg3v(c3, kb)
            if 0 <= c2 < NC2:
                seg2v(c2, kb)
            if 0 <= c1 < NC2:
                seg1(c1)
            if 0 <= c0 < NC2:
                seg0(c0)


        slot_ap = [psq.ap()[:, 0:512], psq.ap()[:, 512:1024],
                   psk.ap()[:, 0:512], psk.ap()[:, 512:1024],
                   ps0.ap()[:, 0:512], ps1.ap()[:, 0:512]]
        slotR = [R(f"sl{i}") for i in range(6)]
        hasm_v, hasm_R = W0.ap()[:, 0:1408], R("hasm")
        yst_R = R("ystore")

        hasm2_v = [W0.ap()[:, 0:1408], f32v(A5, 8192, 1408)]
        hasm2_R = [R("hasm0"), R("hasm1")]
        pending_q = []

        def ldw_gu(h, rh, woff):
            for r4 in range(4):
                r = rh * 4 + r4
                src = _dram_view(
                    wsh_all, r * SHARD_ELEMS + woff,
                    [(SH_GU, 128), (128 * SH_GU, 8), (1, SH_GU)])
                h.dma_start(
                    out=wf_v[:, r4 * 2816:(r4 + 1) * 2816]
                    .rearrange("p (kc j) -> p kc j", j=352),
                    in_=src)

        def emit_G(tc, rh):
            for r4 in range(4):
                sl = (tc * 4 + r4) % 6

                def gmm(h, tc=tc, r4=r4, sl=sl):
                    out = slot_ap[sl][:, 0:352]
                    last = None
                    for kc in range(8):
                        last = h.matmul(
                            out,
                            x2qt_v[:, tc * 1024 + kc * 128:
                                   tc * 1024 + (kc + 1) * 128],
                            wf_v[:, r4 * 2816 + kc * 352:
                                 r4 * 2816 + (kc + 1) * 352],
                            start=(kc == 0), stop=(kc == 7))
                    return last
                kb.stage("tensor", gmm, reads=[x2qt_R, wf_R],
                         writes=[slotR[sl]])
                idx = tc * 1408 + r4 * 352
                kb.stage("scalar", lambda s, sl=sl, idx=idx:
                         s.activation(out=sgb_v[:, idx:idx + 352],
                                      in_=slot_ap[sl][:, 0:352],
                                      func=AF.Silu),
                         reads=[slotR[sl]], writes=[sgb_R])

        def emit_U(tc, rh):
            hp = tc % 2
            for r4 in range(4):
                sl = (tc * 4 + r4) % 6

                def umm(h, tc=tc, r4=r4, sl=sl):
                    out = slot_ap[sl][:, 0:352]
                    last = None
                    for kc in range(8):
                        last = h.matmul(
                            out,
                            x2qt_v[:, tc * 1024 + kc * 128:
                                   tc * 1024 + (kc + 1) * 128],
                            wf_v[:, r4 * 2816 + kc * 352:
                                 r4 * 2816 + (kc + 1) * 352],
                            start=(kc == 0), stop=(kc == 7))
                    return last
                kb.stage("tensor", umm, reads=[x2qt_R, wf_R],
                         writes=[slotR[sl]])
                idx = tc * 1408 + r4 * 352
                kb.stage("vector", lambda v, sl=sl, idx=idx, r4=r4, hp=hp:
                         v.tensor_tensor(
                             hasm2_v[hp][:, r4 * 352:(r4 + 1) * 352],
                             slot_ap[sl][:, 0:352],
                             sgb_v[:, idx:idx + 352], ALU.mult),
                         reads=[slotR[sl], sgb_R], writes=[hasm2_R[hp]])

        def emit_Qa(tc, rh):
            hp = tc % 2
            hq_t = hq_v[hp]
            hqR = hq_R[hp]
            emit_quant_multi(kb, [
                (128, 512, hasm2_v[hp][:, 0:512], hasm2_R[hp],
                 hq_t[:, 0:512], hqR, qsA, ("ss", "q"), True),
                (128, 512, hasm2_v[hp][:, 512:1024], hasm2_R[hp],
                 hq_t[:, 512:1024], hqR, qsB, ("ss", "q"), True),
                (128, 384, hasm2_v[hp][:, 1024:1408], hasm2_R[hp],
                 hq_t[:, 1024:1408], hqR, qs3, ("ss", "q"), True)])

        def emit_Qb(tc, rh):
            hp = tc % 2
            hq_t = hq_v[hp]
            hqR = hq_R[hp]
            for base in range(0, 11, 8):
                nn = min(8, 11 - base)
                half = (base // 8) % 2

                def trh(h, base=base, nn=nn, half=half, hq_t=hq_t):
                    last = None
                    for j in range(nn):
                        last = h.transpose(
                            pst.ap()[:, half * 1024 + j * 128:
                                     half * 1024 + (j + 1) * 128],
                            hq_t[:, (base + j) * 128:(base + j + 1) * 128],
                            ident)
                    return last
                kb.stage("tensor", trh, reads=[hqR, oi_R],
                         writes=[pst_R[half]])
                base_col = rh * 11264 + base * 1024 + tc * 128

                def evh(s, base_col=base_col, nn=nn, half=half):
                    dst = hqt[:, base_col:base_col + nn * 1024] \
                        .rearrange("p (j q) -> p j q", q=1024)[:, :, 0:128]
                    return s.copy(
                        dst,
                        pst.ap()[:, half * 1024:half * 1024 + nn * 128]
                        .rearrange("p (j q) -> p j q", q=128))
                kb.stage("scalar", evh, reads=[pst_R[half]], writes=[hqt_R])

        for th in range(2):
            kb.stage("sync", lambda h, th=th: h.dma_start(
                out=x2qt_v.rearrange("p (t f) -> p t f", f=1024),
                in_=x2qt_scr.ap()[th * 1024:(th + 1) * 1024, :]
                .rearrange("(t p) f -> p t f", p=128)),
                reads=[x2qscr_R[th * 8 + i] for i in range(8)],
                writes=[x2qt_R], dma=True)
            for rh in range(2):
                kb.stage("sync", lambda h, rh=rh: ldw_gu(h, rh, 0),
                         reads=[wshall_R], writes=[wf_R], dma=True, n_dma=4)
                for tc in range(8):
                    emit_G(tc, rh)
                    if tc == 0 and pending_q:
                        pending_q.pop(0)()
                kb.stage("sync", lambda h, rh=rh: ldw_gu(h, rh, WU_OFF),
                         reads=[wshall_R], writes=[wf_R], dma=True, n_dma=4)
                for tc in range(8):
                    emit_U(tc, rh)
                    if tc > 0:
                        emit_Qa(tc - 1, rh)
                    if tc > 1:
                        emit_Qb(tc - 2, rh)

                def q_tail(rh=rh):
                    emit_Qa(7, rh)
                    emit_Qb(6, rh)
                    emit_Qb(7, rh)
                pending_q.append(q_tail)
            while pending_q:
                pending_q.pop(0)()
            for nt in range(2):
                def ldwd(h, nt=nt):
                    for r4 in range(4):
                        r = nt * 4 + r4
                        src = _dram_view(
                            wsh_all, r * SHARD_ELEMS + WD_OFF,
                            [(SH_D, 128), (128 * SH_D, 22), (1, 128)])
                        dst = wf_v[:, 0:11264] \
                            .rearrange("p (kc q) -> p kc q", q=512) \
                            [:, :, r4 * 128:(r4 + 1) * 128]
                        h.dma_start(out=dst, in_=src)
                kb.stage("sync", ldwd, reads=[wshall_R], writes=[wf_R],
                         dma=True, n_dma=4)
                for tc in range(8):
                    sl = [0, 2, 4, 5][tc % 4]

                    def dmm(h, tc=tc, sl=sl):
                        out = slot_ap[sl]
                        last = None
                        for kc in range(22):
                            last = h.matmul(
                                out,
                                hqt[:, kc * 1024 + tc * 128:
                                    kc * 1024 + (tc + 1) * 128],
                                wf_v[:, kc * 512:(kc + 1) * 512],
                                start=(kc == 0), stop=(kc == 21))
                        return last
                    kb.stage("tensor", dmm, reads=[hqt_R, wf_R],
                             writes=[slotR[sl]])
                    pcc = tc % 2
                    kb.stage("sync", lambda h, th=th, tc=tc, pcc=pcc:
                             h.dma_start(
                                 out=x2r_v[pcc],
                                 in_=x2_scr.ap()[th * 1024 + tc * 128:
                                                 th * 1024 + (tc + 1) * 128, :]),
                             reads=[x2scr_R[th * 8 + tc]],
                             writes=[x2r_R[pcc]], dma=True)
                    kb.stage("vector", lambda v, sl=sl, pcc=pcc, nt=nt:
                             v.scalar_tensor_tensor(
                                 y_v[:, nt * 512:(nt + 1) * 512],
                                 slot_ap[sl], 1.0,
                                 x2r_v[pcc][:, nt * 512:(nt + 1) * 512],
                                 ALU.mult, ALU.add),
                             reads=[slotR[sl], x2r_R[pcc]], writes=[y_R])
                    kb.stage("sync", lambda h, th=th, tc=tc, nt=nt:
                             h.dma_start(
                                 out=y_out.ap()[th * 1024 + tc * 128:
                                                th * 1024 + (tc + 1) * 128,
                                                nt * 512:(nt + 1) * 512],
                                 in_=y_v[:, nt * 512:(nt + 1) * 512]),
                             reads=[y_R], writes=[yst_R], dma=True)
        kb.barrier()
        kb.finalize()
    return nc, dbg_outs


def make_in_maps(inputs):
    x = np.ascontiguousarray(
        np.asarray(inputs["L0_attn_hidden_states"], dtype=np.float32))
    consts = _host_consts()
    gam = consts["gam"]
    perm_block = _perm_qk()
    perm = np.concatenate([perm_block, 1024 + perm_block,
                           2048 + np.arange(2048)])
    wqkvp = np.ascontiguousarray(
        np.asarray(inputs["L0_attn_w_qkv"], np.float32)[:, perm])
    b_qkv_p = np.asarray(inputs["L0_attn_b_qkv"], np.float32)[perm]
    biases = np.ascontiguousarray(np.concatenate(
        [b_qkv_p, np.asarray(inputs["L0_attn_b_g"], np.float32),
         np.asarray(inputs["L0_attn_b_p"], np.float32)])[None, :])
    wg = np.ascontiguousarray(np.asarray(inputs["L0_attn_w_g"], np.float32))
    wp = np.ascontiguousarray(np.asarray(inputs["L0_attn_w_p"], np.float32))
    wgate = np.ascontiguousarray(np.asarray(inputs["L0_ffn_w_gate"], np.float32))
    wup = np.ascontiguousarray(np.asarray(inputs["L0_ffn_w_up"], np.float32))
    wdown = np.ascontiguousarray(np.asarray(inputs["L0_ffn_w_down"], np.float32))
    wdown_t = np.ascontiguousarray(wdown.T)
    in_maps = []
    for core in range(NCORE):
        cc = _core_consts(core, gam)
        bi, t0 = cc["bi"], cc["t0"]
        r0 = core * SH_GU
        in_maps.append({
            "xsl": np.ascontiguousarray(x[bi, t0:t0 + TLOC, :]),
            "cs_in": np.ascontiguousarray(cc["cs"]),
            "wqkvp": wqkvp,
            "wg_in": wg,
            "wp_in": wp,
            "wgate": np.ascontiguousarray(wgate[r0:r0 + SH_GU, :]),
            "wup": np.ascontiguousarray(wup[r0:r0 + SH_GU, :]),
            "wdown": np.ascontiguousarray(
                wdown[core * SH_D:(core + 1) * SH_D, :]),
            "bia_in": biases,
            "dtp_in": np.ascontiguousarray(consts["dtp"]),
            "qdd_in": np.ascontiguousarray(consts["qddec"]),
            "gci_in": np.ascontiguousarray(consts["gci"]),
            "coef_in": np.ascontiguousarray(cc["coef"]),
            "oi_in": np.ascontiguousarray(consts["oi"]),
        })
    return in_maps


def kernel(**inputs):
    nc, _ = build_nc(dbg=False)
    in_maps = make_in_maps(inputs)
    res = run_bass_kernel_spmd(nc, in_maps, core_ids=list(range(NCORE)))
    y = np.empty((B, T, E), dtype=np.float32)
    for core in range(NCORE):
        bi, ci = core // 4, core % 4
        y[bi, ci * TLOC:(ci + 1) * TLOC, :] = res.results[core]["y"]
    return y


# version 37
# speedup vs baseline: 1.1330x; 1.1330x over previous
import dataclasses
import numpy as np
import ml_dtypes
from contextlib import ExitStack

import concourse.bass as bass
import concourse.mybir as mybir
from concourse.bass_utils import run_bass_kernel_spmd

F32 = mybir.dt.float32
BF16 = mybir.dt.bfloat16
U32 = mybir.dt.uint32
AF = mybir.ActivationFunctionType
ALU = mybir.AluOpType
AXX = mybir.AxisListType.X
bfdt = ml_dtypes.bfloat16

B, T, E = 2, 8192, 1024
H, dk, V, dv = 8, 128, 2048, 256
CH = 128
TLOC = 2048
NCH = TLOC // CH
IF = 2816
NCORE = 8
M_FIX = float(np.float32(1.5 * 2 ** 23))
K_VELT = float(np.float32(2 ** 22 + 1))
LN_EPS = 1e-5
GN_EPS = 1e-5
SH_GU = 352
SH_D = 128
SHARD_ELEMS = SH_GU * 1024 * 2 + SH_D * IF
WU_OFF = SH_GU * 1024
WD_OFF = SH_GU * 1024 * 2


class Res:
    __slots__ = ("name", "last_w", "readers")

    def __init__(self, name, last_w=()):
        self.name = name
        self.last_w = list(last_w)
        self.readers = []


class _DmaProxy:

    def __init__(self, h, incs):
        self._h = h
        self._incs = list(incs)
        self._i = 0

    def __getattr__(self, a):
        return getattr(self._h, a)

    def dma_start(self, **kw):
        assert self._i < len(self._incs), "stage issued more DMAs than n_dma"
        sem = self._incs[self._i]
        self._i += 1
        inst = self._h.dma_start(**kw)
        inst.then_inc(sem, 16)
        return inst


class KB:

    ENGS = ("tensor", "vector", "scalar", "gpsimd", "sync")
    NLANES = {"sync": 8, "gpsimd": 4, "scalar": 2, "vector": 1, "tensor": 1}

    def __init__(self, nc, ctx, block, n_cc=2):
        self.nc = nc
        self.block = block
        self.sems = {}
        self.cnt = {}
        self.hwm = {}
        self.lanesems = {}
        self.lane_rr = {}
        self.nlanes = {}
        self.pending = {e: [] for e in self.ENGS}
        for e in self.ENGS:
            self.sems[e] = ctx.enter_context(nc.semaphore(f"kb_{e}"))
            self.cnt[e] = 0
            self.hwm[e] = {}
            nl = self.NLANES[e]
            self.nlanes[e] = nl
            self.lane_rr[e] = 0
            self.lanesems[e] = []
            for i in range(nl):
                sm = ctx.enter_context(nc.semaphore(f"kb_{e}_l{i}"))
                self.lanesems[e].append(sm)
                self.sems[f"{e}.l{i}"] = sm
                self.cnt[f"{e}.l{i}"] = 0
        self.ccsems = [ctx.enter_context(nc.semaphore(f"kb_cc{i}"))
                       for i in range(n_cc)]
        for i in range(n_cc):
            self.sems[f"cc{i}"] = self.ccsems[i]
            self.cnt[f"cc{i}"] = 0
        self.cc_idx = 0

    def _deps(self, reads, writes, extra):
        deps = {}

        def add(tok):
            if tok is None:
                return
            sm, v = tok
            if deps.get(sm, -1) < v:
                deps[sm] = v

        for r in reads:
            for t in r.last_w:
                add(t)
        for w in writes:
            for t in w.last_w:
                add(t)
            for t in w.readers:
                add(t)
        for t in extra:
            add(t)
        return deps

    def _wait_list(self, eng, deps):
        hwm = self.hwm[eng]
        waits = []
        for sname in sorted(deps):
            v = deps[sname]
            if sname == eng == "tensor":
                continue
            if hwm.get(sname, 0) >= v:
                continue
            waits.append((self.sems[sname], v))
            hwm[sname] = v
        return waits

    def stage(self, eng, fn, reads=(), writes=(), extra=(), dma=False,
              n_dma=1, cc=False):
        deps = self._deps(reads, writes, extra)
        waits = self._wait_list(eng, deps)
        if dma:
            incs = []
            toks = []
            hwm = self.hwm[eng]
            nl = self.nlanes[eng]
            for _ in range(n_dma):
                lane = self.lane_rr[eng] % nl
                self.lane_rr[eng] += 1
                lname = f"{eng}.l{lane}"
                prev = self.cnt[lname]
                if prev > 0 and hwm.get(lname, 0) < prev:
                    waits.append((self.sems[lname], prev))
                    hwm[lname] = prev
                self.cnt[lname] = prev + 16
                incs.append(self.lanesems[eng][lane])
                toks.append((lname, prev + 16))
            self.pending[eng].append(("dma", waits, fn, incs, n_dma))
        elif cc:
            i = self.cc_idx
            self.cc_idx += 1
            self.cnt[f"cc{i}"] = 1
            toks = [(f"cc{i}", 1)]
            self.pending[eng].append(("cc", waits, fn, self.ccsems[i], 1))
        else:
            self.cnt[eng] += 1
            toks = [(eng, self.cnt[eng])]
            self.pending[eng].append(("op", waits, fn, self.sems[eng], 1))
        for r in reads:
            r.readers.extend(toks)
        for w in writes:
            w.last_w = list(toks)
            w.readers = []
        return toks

    def barrier(self):
        toks = [(sname, v) for sname, v in self.cnt.items()
                if v > 0 and not sname.startswith("cc")]
        for e in self.ENGS:
            deps = {}
            for sname, v in toks:
                if sname == e:
                    continue
                deps[sname] = v
            waits = self._wait_list(e, deps)
            if waits:
                self.pending[e].append(("wait", waits, None, None, 0))
        return toks

    def finalize(self):
        for e in self.ENGS:
            entries = self.pending[e]

            def play(h, _entries=entries):
                for kind, waits, fn, sem, n in _entries:
                    for sm, v in waits:
                        h.wait_ge(sm, v)
                    if kind == "wait":
                        continue
                    if kind == "dma":
                        proxy = _DmaProxy(h, sem)
                        fn(proxy)
                        assert proxy._i == n, \
                            f"dma stage issued {proxy._i} != declared {n}"
                    elif kind == "cc":
                        inst = fn(h)
                        inst.then_inc(sem, 1)
                    else:
                        inst = fn(h)
                        assert inst is not None
                        inst.then_inc(sem, 1)
            getattr(self.block, e)(play)
        self.pending = {e: [] for e in self.ENGS}


def _gammas():
    s_, e_ = np.log(1 / 32.0), np.log(1 / 512.0)
    return 1.0 - np.exp(np.linspace(s_, e_, H))


GAM_PY = _gammas()


def _perm_qk():
    return np.concatenate(
        [np.concatenate([128 * h + np.arange(0, 128, 2),
                         128 * h + np.arange(1, 128, 2)]) for h in range(H)])


def _cs_tables(pos, gam):
    D = 1024
    half = 512
    fidx = np.arange(half)
    scale_vec = (2 * fidx + 0.4 * D) / (1.4 * D)
    scale = scale_vec[None, :] ** (pos[:, None] / 512.0)
    inv_freq = 1.0 / 10000.0 ** (fidx / half)
    sinu = pos[:, None] * inv_freq[None, :]
    sin, cos = np.sin(sinu), np.cos(sinu)
    cq, sq = cos * scale, sin * scale
    ck, sk = cos / scale, sin / scale
    tau = (pos.astype(np.int64) % CH).astype(np.float64)
    dec = np.zeros((len(pos), half))
    for h in range(H):
        dec[:, 64 * h:64 * (h + 1)] = (gam[h] ** (127 - tau))[:, None]
    return (cq.astype(np.float32), sq.astype(np.float32),
            (ck * dec).astype(np.float32), (sk * dec).astype(np.float32))


def _host_consts():
    gam = GAM_PY
    tau = np.arange(CH, dtype=np.float64)
    qddec = np.zeros((CH, 1024))
    dtp = np.zeros((CH, H * CH))
    gci = np.zeros((CH, H * CH))
    eye = np.eye(CH)
    tq = np.arange(CH)[None, :]
    tk = np.arange(CH)[:, None]
    for h in range(H):
        qddec[:, 128 * h:128 * (h + 1)] = (gam[h] ** (tau + 1))[:, None]
        dtp[:, 128 * h:128 * (h + 1)] = (tq >= tk) * gam[h] ** (tq - 127.0)
        gci[:, 128 * h:128 * (h + 1)] = eye * (gam[h] ** CH)
    oi = np.zeros((128, 256), dtype=np.float32)
    oi[:, 0:128] = 1.0
    oi[:, 128:256] = np.eye(128)
    return dict(gam=gam, qddec=qddec.astype(bfdt), dtp=dtp.astype(bfdt),
                gci=gci.astype(bfdt), oi=oi.astype(bfdt))


def _core_consts(core, gam):
    bi, ci = core // 4, core % 4
    t0 = ci * TLOC
    pos = np.arange(t0, t0 + TLOC, dtype=np.float64)
    cq, sq, ckf, skf = _cs_tables(pos, gam)
    cs = np.concatenate([cq, sq, ckf, skf], axis=1).astype(bfdt)
    coef = np.zeros((8, 8))
    for r in range(8):
        if r // 4 == bi and (r % 4) < ci:
            for h in range(H):
                coef[r, h] = gam[h] ** (TLOC * (ci - 1 - (r % 4)))
    coef_bc = np.tile(coef.reshape(1, 64), (128, 1)).astype(np.float32)
    return dict(cs=cs, coef=coef_bc, bi=bi, t0=t0)


def _dram_view(t, offset, dims):
    ap = t.ap()
    return dataclasses.replace(ap, offset=offset, ap=[[s, n] for s, n in dims])


def quant_ops(P, n, x_ap, x_res, out_ap, out_res, sc, pool=(),
              sign_act=False, abs_dve=False):
    nb = n // 16

    def b3(ap):
        return ap.rearrange("p (b k) -> p b k", k=16)

    def bc3(ap):
        return ap.rearrange("p (b o) -> p b o", o=1).broadcast_to((P, nb, 16))

    am, amR = sc["am"]
    sc6, scR = sc["sc6"]
    rs6, rsR = sc["rs6"]
    am = am[0:P, 0:nb]
    sc6 = sc6[0:P, 0:nb]
    rs6 = rs6[0:P, 0:nb]
    ax, axR = sc["ax"]
    un, unR = sc["un"]
    M2, M2R = sc["M2"]
    sb, sbR = sc["sb"]
    ax = ax[0:P, 0:n]
    un = un[0:P, 0:n]
    M2 = M2[0:P, 0:n]
    sb = sb[0:P, 0:n]

    ops = []
    ops.append(("vector", lambda v: v.tensor_reduce(
        am, b3(x_ap), AXX, ALU.max, apply_absolute_value=True),
        [x_res], [amR]))
    ops.append(("vector", lambda v: v.tensor_scalar(
        sc6, am, 6e-12, 1.0 / 12.0, ALU.max, ALU.mult), [amR], [scR]))
    ops.append(("vector", lambda v: v.reciprocal(rs6, sc6), [scR], [rsR]))
    if abs_dve:
        ops.append(("vector", lambda v: v.tensor_scalar(
            ax.bitcast(U32), x_ap.bitcast(U32), 0x7FFFFFFF, None,
            ALU.bitwise_and), [x_res], [axR]))
    else:
        ops.append(("scalar", lambda a: a.activation(
            out=ax, in_=x_ap, func=AF.Abs), [x_res], [axR]))
    if sign_act:
        ops.append(("scalar", lambda a: a.activation(
            out=sb, in_=x_ap, func=AF.Sign), [x_res], [sbR]))
    else:
        ops.append(("vector", lambda v: v.tensor_scalar(
            sb.bitcast(U32), x_ap.bitcast(U32), 0x80000000, None,
            ALU.bitwise_and), [x_res], [sbR]))
    ops.append(("gpsimd" if "un" in pool else "vector",
                lambda v: v.tensor_tensor(
        b3(un), b3(ax), bc3(rs6), ALU.mult), [axR, rsR], [unR]))
    ops.append(("vector", lambda v: v.tensor_scalar(
        ax, un, 4.0, M_FIX, ALU.min, ALU.add), [unR], [axR]))
    ops.append(("vector", lambda v: v.tensor_scalar(
        M2, un, 4.0, None, ALU.max), [unR], [M2R]))
    ops.append(("vector", lambda v: v.tensor_scalar(
        ax, ax, -(M_FIX + 4.0), None, ALU.add), [axR], [axR]))
    ops.append(("vector", lambda v: v.tensor_scalar(
        un, M2, K_VELT, None, ALU.mult), [M2R], [unR]))
    ops.append(("vector", lambda v: v.tensor_tensor(
        M2, un, M2, ALU.subtract), [unR, M2R], [M2R]))
    ops.append(("vector", lambda v: v.tensor_tensor(
        un, un, M2, ALU.subtract), [unR, M2R], [unR]))
    ops.append(("vector", lambda v: v.tensor_tensor(
        ax, ax, un, ALU.add), [axR, unR], [axR]))
    if sign_act:
        ops.append(("gpsimd" if "ss" in pool else "vector",
                    lambda v: v.tensor_tensor(
            b3(sb), b3(sb), bc3(sc6), ALU.mult), [sbR, scR], [sbR]))
    else:
        ops.append(("vector", lambda v: v.tensor_tensor(
            b3(sb).bitcast(U32), b3(sb).bitcast(U32),
            bc3(sc6).bitcast(U32), ALU.bitwise_or), [sbR, scR], [sbR]))
    ops.append(("gpsimd" if "q" in pool else "vector",
                lambda v: v.tensor_tensor(
        out_ap, ax, sb, ALU.mult), [axR, sbR], [out_res]))
    return ops


def emit_quant_multi(kb, calls):
    lists = [quant_ops(*c) for c in calls]
    for i in range(max(len(lst) for lst in lists)):
        for lst in lists:
            if i < len(lst):
                eng, fn, rd, wr = lst[i]
                kb.stage(eng, fn, reads=rd, writes=wr)


class KQ:

    def __init__(self):
        self.calls = []

    def stage(self, eng, fn, reads=(), writes=(), extra=(), dma=False,
              n_dma=1, cc=False):
        self.calls.append((eng, fn, reads, writes, extra, dma, n_dma, cc))


def kq_merge(kb, queues):
    n = max((len(q.calls) for q in queues), default=0)
    for i in range(n):
        for q in queues:
            if i < len(q.calls):
                eng, fn, rd, wr, ex, dma, n_dma, cc = q.calls[i]
                kb.stage(eng, fn, reads=rd, writes=wr, extra=ex, dma=dma,
                         n_dma=n_dma, cc=cc)


RSQRT_MAGIC = float(np.uint32(0x5F3759DF).view(np.float32))


def emit_rsqrt(kb, w_ap, w_res, out_ap, out_res, ta, taR, tb, tbR, mg, mgR,
               n, iters=2):
    kb.stage("vector", lambda v: v.tensor_scalar(
        ta.bitcast(U32), w_ap.bitcast(U32), 1, None, ALU.logical_shift_right),
        reads=[w_res], writes=[taR])
    kb.stage("vector", lambda v: v.tensor_tensor(
        ta.bitcast(U32), mg.bitcast(U32).broadcast_to((128, n)),
        ta.bitcast(U32), ALU.subtract),
        reads=[taR, mgR], writes=[taR])
    for it in range(iters):
        last = it == iters - 1
        kb.stage("vector", lambda v: v.tensor_tensor(tb, ta, ta, ALU.mult),
                 reads=[taR], writes=[tbR])
        kb.stage("vector", lambda v: v.tensor_tensor(tb, tb, w_ap, ALU.mult),
                 reads=[tbR, w_res], writes=[tbR])
        kb.stage("vector", lambda v: v.tensor_scalar(
            tb, tb, -0.5, 1.5, ALU.mult, ALU.add),
            reads=[tbR], writes=[tbR])
        dst, dstR = (out_ap, out_res) if last else (ta, taR)
        kb.stage("vector", lambda v, dst=dst: v.tensor_tensor(
            dst, ta, tb, ALU.mult),
            reads=[taR, tbR], writes=[dstR])
    return out_res


def build_nc(dbg=False, stop_after=9, nch1=NCH, lite0=False):
    nc = bass.Bass("TRN2", target_bir_lowering=False, debug=False)

    xsl = nc.dram_tensor("xsl", [TLOC, E], F32, kind="ExternalInput")
    cs_in = nc.dram_tensor("cs_in", [TLOC, 2048], BF16, kind="ExternalInput")
    wqkvp = nc.dram_tensor("wqkvp", [E, 4096], F32, kind="ExternalInput")
    wg_in = nc.dram_tensor("wg_in", [E, 2048], F32, kind="ExternalInput")
    wp_in = nc.dram_tensor("wp_in", [V, E], F32, kind="ExternalInput")
    wgate = nc.dram_tensor("wgate", [SH_GU, E], F32, kind="ExternalInput")
    wup = nc.dram_tensor("wup", [SH_GU, E], F32, kind="ExternalInput")
    wdown = nc.dram_tensor("wdown", [SH_D, IF], F32, kind="ExternalInput")
    bia_in = nc.dram_tensor("bia_in", [1, 7168], F32, kind="ExternalInput")
    dtp_in = nc.dram_tensor("dtp_in", [128, 1024], BF16, kind="ExternalInput")
    qdd_in = nc.dram_tensor("qdd_in", [128, 1024], BF16, kind="ExternalInput")
    gci_in = nc.dram_tensor("gci_in", [128, 1024], BF16, kind="ExternalInput")
    coef_in = nc.dram_tensor("coef_in", [128, 64], F32, kind="ExternalInput")
    oi_in = nc.dram_tensor("oi_in", [128, 256], BF16, kind="ExternalInput")

    y_out = nc.dram_tensor("y", [TLOC, E], F32, kind="ExternalOutput")

    ret_scr = nc.dram_tensor("ret_scr", [TLOC, V], F32)
    x2_scr = nc.dram_tensor("x2_scr", [TLOC, E], F32)
    x2qt_scr = nc.dram_tensor("x2qt_scr", [TLOC, E], BF16)
    qdt_scr = nc.dram_tensor("qdt_scr", [TLOC, E], BF16)
    xfms_scr = nc.dram_tensor("xfms_scr", [TLOC, E], BF16)
    sg_scr = nc.dram_tensor("sg_scr", [TLOC, V], BF16)
    wsh_in = nc.dram_tensor("wsh_in", [SHARD_ELEMS // 128, 128], BF16)
    wsh_all = nc.dram_tensor("wsh_all", [NCORE * SHARD_ELEMS // 128, 128], BF16)
    st_in = nc.dram_tensor("st_in", [128, 2048], BF16)
    st_all = nc.dram_tensor("st_all", [NCORE * 128, 2048], BF16)

    dbg_outs = {}
    if dbg:
        for nm, sh in [("d_qkv", [128, 4096]), ("d_P", [128, 2048]),
                       ("d_sext", [128, 2048]), ("d_ret", [128, 2048]),
                       ("d_g", [128, 2048]), ("d_x2", [128, 1024]),
                       ("d_x2q", [128, 1024]), ("d_sg", [128, 2048])]:
            dbg_outs[nm] = nc.dram_tensor(nm, sh, F32, kind="ExternalOutput")

    with ExitStack() as ctx:
        ent = ctx.enter_context

        T1 = nc.alloc_sbuf_tensor("T1", [128, 32768], BF16)
        T2 = nc.alloc_sbuf_tensor("T2", [128, 16384], BF16)
        A5 = nc.alloc_sbuf_tensor("A5", [128, 15360], BF16)
        A6 = nc.alloc_sbuf_tensor("A6", [128, 12288], BF16)
        QS = nc.alloc_sbuf_tensor("QS", [128, 4 * 1024 + 3 * 88], F32)
        SS = nc.alloc_sbuf_tensor("SS", [128, 1536], F32)
        SSB = nc.alloc_sbuf_tensor("SSB", [128, 128], BF16)
        MG = nc.alloc_sbuf_tensor("MG", [128, 1], F32)
        CQ = nc.alloc_sbuf_tensor("CQ", [128, 8], F32)
        W0 = nc.alloc_sbuf_tensor("W0", [128, 1408], F32)
        W0Q = nc.alloc_sbuf_tensor("W0Q", [128, 1024], BF16)
        W0S = nc.alloc_sbuf_tensor("W0S", [128, 1024], BF16)
        DTP = nc.alloc_sbuf_tensor("DTP", [128, 1024], BF16)
        QDD = nc.alloc_sbuf_tensor("QDD", [128, 1024], BF16)
        GCI = nc.alloc_sbuf_tensor("GCI", [128, 1024], BF16)
        COEF = nc.alloc_sbuf_tensor("COEF", [128, 64], F32)
        OI = nc.alloc_sbuf_tensor("OI", [128, 256], BF16)
        NB2 = nc.alloc_sbuf_tensor("NB2", [2, 2048], BF16)
        CB2 = nc.alloc_sbuf_tensor("CB2", [2, 7168], BF16)
        assert nc.sbuf_bytes_remaining >= 0

        psq = nc.alloc_psum_tensor("psq", [128, 1024], F32)
        psk = nc.alloc_psum_tensor("psk", [128, 1024], F32)
        pst = nc.alloc_psum_tensor("pst", [128, 2048], BF16)
        ps0 = nc.alloc_psum_tensor("ps0", [128, 512], F32)
        ps1 = nc.alloc_psum_tensor("ps1", [128, 512], F32)

        block = ent(nc.Block())
        kb = KB(nc, ctx, block)
        R = Res

        def bfv(t, c0, n):
            return t.ap()[:, c0:c0 + n]

        def f32v(t, c0, n):
            return t.ap()[:, c0:c0 + 2 * n].bitcast(F32)

        wqkv_t = [bfv(T1, kc * 4096, 4096) for kc in range(8)]
        wqkv_R = R("wqkv")
        retp_v = [f32v(T1, 0, 2048), f32v(T1, 4096, 2048)]
        retp_R = [R("retp0"), R("retp1")]
        g_v, g_R = f32v(T1, 8192, 2048), R("g")
        xc_v = [f32v(T1, 12288, 1024), f32v(T1, 14336, 1024)]
        xc_R = [R("xc0"), R("xc1")]
        x2_v, x2_R = f32v(T1, 16384, 1024), R("x2")
        sgin_v = [bfv(T1, 20480, 2048), bfv(T1, 22528, 2048)]
        sgin_R = [R("sgin0"), R("sgin1")]
        norm_v, norm_R = bfv(T1, 24576, 2048), R("normed")
        gated_v, gated_R = bfv(T1, 26624, 2048), R("gated")
        gatT_v, gatT_R = bfv(T1, 28672, 2048), R("gatedT")
        x2q_v, x2q_R = bfv(T1, 30720, 1024), R("x2q")
        qdin_v, qdin_R = bfv(T1, 31744, 1024), R("qdin")
        hqt = T1.ap()
        hqt_R = R("hqt")
        y_v, y_R = f32v(T1, 22528, 1024), R("ysb")
        x2r_v = [f32v(T1, 24576, 1024), f32v(T1, 26624, 1024)]
        x2r_R = [R("x2r0"), R("x2r1")]
        hq_v = [bfv(T1, 28672, 1408), bfv(T1, 30080, 1408)]
        hq_R = [R("hq0"), R("hq1")]

        wg_t = [bfv(T2, kc * 2048, 2048) for kc in range(8)]
        wg_R = R("wg")
        wp_R = R("wp")
        wp_t = ([bfv(A6, kc * 1024, 1024) for kc in range(4)]
                + [bfv(A6, 6144 + kc * 1024, 1024) for kc in range(4)]
                + [DTP.ap()[:, :], QDD.ap()[:, :], GCI.ap()[:, :]]
                + [W0.ap().bitcast(BF16)[:, 0:1024],
                   W0.ap().bitcast(BF16)[:, 1024:2048],
                   W0Q.ap()[:, :], W0S.ap()[:, :],
                   bfv(A5, 14336, 1024)])
        wf_v, wf_R = bfv(T2, 0, 11264), wg_R

        cs_v = [bfv(A5, 0, 2048), bfv(A5, 2048, 2048)]
        cs_R = [R("cs0"), R("cs1")]
        xb_v = [bfv(A5, 4096, 1024), bfv(A5, 5120, 1024)]
        xb_R = [R("xb0"), R("xb1")]
        xfc_v = [bfv(A5, 6144, 1024), bfv(A5, 7168, 1024)]
        xfc_R = [R("xfc0"), R("xfc1")]
        xfm_v, xfm_R = bfv(A5, 8192, 1024), R("xfm")
        xsq_v, xsq_R = bfv(A5, 9216, 1024), R("xsq")
        qsb_v, qsb_R = bfv(A5, 10240, 1024), R("qsb")
        ksb_v, ksb_R = bfv(A5, 11264, 1024), R("ksb")
        qpr_v, qpr_R = bfv(A5, 12288, 1024), R("qpr")
        kdc_v, kdc_R = bfv(A5, 13312, 1024), R("kdc")
        qdt_v, qdt_R = bfv(A5, 14336, 1024), R("qdtok")
        qfm_v, qfm_R = xfm_v, xfm_R
        kfm_v, kfm_R = xsq_v, xsq_R
        scr_v, scr_R = bfv(A5, 10240, 1024), qsb_R
        qdstg_v, qdstg_R = bfv(A5, 11264, 1024), ksb_R
        x2qt_v, x2qt_R = bfv(A5, 0, 8192), R("x2qtstage")

        vc_v, vc_R = bfv(A6, 0, 2048), R("vc")
        P_v = bfv(A6, 2048, 2048)
        P_half = [P_v[:, 0:1024], P_v[:, 1024:2048]]
        P_R = [R("P0"), R("P1")]
        sext_v, sext_R = bfv(A6, 4096, 2048), R("sext")
        ret_v, ret_R = f32v(A6, 6144, 2048), R("retsb")
        sgtmp_v, sgtmp_R = bfv(A6, 0, 2048), R("sgtmp")
        sgstg_v, sgstg_R = bfv(A6, 10240, 2048), R("sgstg")
        sgb_v, sgb_R = bfv(A6, 0, 11264), R("sgb")

        qsap = {}
        off = 0
        for nm, n in [("ax", 1024), ("un", 1024), ("M2", 1024), ("sb", 1024),
                      ("am", 88), ("sc6", 88), ("rs6", 88)]:
            qsap[nm] = QS.ap()[:, off:off + n]
            off += n

        def _mk_qs(tag, c0, c1, b0, b1):
            d = {}
            for nm in ("ax", "un", "M2", "sb"):
                d[nm] = (qsap[nm][:, c0:c1], R(tag + "_" + nm))
            for nm in ("am", "sc6", "rs6"):
                d[nm] = (qsap[nm][:, b0:b1], R(tag + "_" + nm))
            return d
        qsA = _mk_qs("qA", 0, 512, 0, 32)
        qsB = _mk_qs("qB", 512, 1024, 32, 64)
        t2f = T2.ap().bitcast(F32)
        qs3 = {}
        o3 = 5632
        for nm, n in [("ax", 384), ("un", 384), ("M2", 384), ("sb", 384),
                      ("am", 24), ("sc6", 24), ("rs6", 24)]:
            qs3[nm] = (t2f[:, o3:o3 + n], R("q3_" + nm))
            o3 += n

        ssv = {}
        off = 0
        for nm, n in [("tmp1", 512), ("tmp2", 512), ("stm", 128), ("stv", 128),
                      ("smsq", 128), ("gn", 64), ("ln2", 64)]:
            ssv[nm] = (SS.ap()[:, off:off + n], R("s_" + nm))
            off += n
        rstd_v, rstd_R = SSB.ap()[:, 0:128], R("rstd")
        mg_v, mg_R = MG.ap()[:, 0:1], R("magic")

        nb2_R = R("nb2")
        cb2_R = R("cb2")
        ones_col = OI.ap()[:, 0:1]
        ident = OI.ap()[:, 128:256]
        oi_R = R("oi")
        con_R = R("consts")
        dtp_R = R("dtp")
        qdd_R = R("qdd")
        gci_R = R("gci")

        psq_R = [R("psq0"), R("psq1")]
        psk_R = [R("psk0"), R("psk1")]
        pst_R = [R("pstA"), R("pstB")]
        ps0_R = R("ps0")
        ps1_R = R("ps1")

        w0_R, w0q_R, w0s_R = R("w0"), R("w0q"), R("w0s")
        wsh_R, wshall_R = R("wsh"), R("wshall")
        stin_R, stall_R = R("stin"), R("stall")

        kb.stage("sync", lambda h: [
            h.dma_start(out=DTP.ap()[:, :], in_=dtp_in.ap()[:, :]),
            h.dma_start(out=QDD.ap()[:, :], in_=qdd_in.ap()[:, :]),
            h.dma_start(out=GCI.ap()[:, :], in_=gci_in.ap()[:, :]),
            h.dma_start(out=COEF.ap()[:, :], in_=coef_in.ap()[:, :]),
            h.dma_start(out=OI.ap()[:, :], in_=oi_in.ap()[:, :]),
        ], writes=[con_R, dtp_R, oi_R, qdd_R, gci_R], dma=True, n_dma=5)
        kb.stage("vector", lambda v: v.memset(NB2.ap()[0:2, :], 1.0),
                 writes=[nb2_R])
        kb.stage("vector", lambda v: v.memset(mg_v, RSQRT_MAGIC),
                 writes=[mg_R])

        kb.stage("gpsimd", lambda h: h.dma_start(
            out=CB2.ap()[1:2, :], in_=bia_in.ap()[:, :]),
            writes=[cb2_R], dma=True)
        kb.stage("vector", lambda v: v.memset(CB2.ap()[0:1, 6144:7168], 0.0),
                 reads=[], writes=[cb2_R])
        for kc in range(8):
            kb.stage("gpsimd", lambda h, kc=kc: h.dma_start(
                out=wqkv_t[kc], in_=wqkvp.ap()[kc * 128:(kc + 1) * 128, :]),
                writes=[wqkv_R], dma=True)
        for kc in range(8):
            kb.stage("gpsimd", lambda h, kc=kc: h.dma_start(
                out=wg_t[kc], in_=wg_in.ap()[kc * 128:(kc + 1) * 128, :]),
                writes=[wg_R], dma=True)
        for nt in range(8):
            def mk_cs(h, nt=nt):
                last = None
                for kc in range(8):
                    last = h.matmul(ps0.ap()[0:1, 0:512], ones_col,
                                    wqkv_t[kc][:, nt * 512:(nt + 1) * 512],
                                    start=(kc == 0), stop=(kc == 7))
                return last
            kb.stage("tensor", mk_cs, reads=[wqkv_R, oi_R], writes=[ps0_R])
            kb.stage("scalar", lambda s, nt=nt: s.copy(
                CB2.ap()[0:1, nt * 512:(nt + 1) * 512], ps0.ap()[0:1, 0:512]),
                reads=[ps0_R], writes=[cb2_R])
        for nt in range(4):
            def mk_csg(h, nt=nt):
                last = None
                for kc in range(8):
                    last = h.matmul(ps1.ap()[0:1, 0:512], ones_col,
                                    wg_t[kc][:, nt * 512:(nt + 1) * 512],
                                    start=(kc == 0), stop=(kc == 7))
                return last
            kb.stage("tensor", mk_csg, reads=[wg_R, oi_R], writes=[ps1_R])
            kb.stage("scalar", lambda s, nt=nt: s.copy(
                CB2.ap()[0:1, 4096 + nt * 512:4096 + (nt + 1) * 512],
                ps1.ap()[0:1, 0:512]),
                reads=[ps1_R], writes=[cb2_R])

        def shard_tile(w_dram, row0, prow, col0, ncols, flat_base, stride_i,
                       shard_col_off):
            kb.stage("sync", lambda h: h.dma_start(
                out=W0.ap()[0:prow, 0:ncols],
                in_=w_dram.ap()[row0:row0 + prow, col0:col0 + ncols]),
                writes=[w0_R], dma=True)
            na = min(512, ncols)
            calls = [(prow, na, W0.ap()[0:prow, 0:na], w0_R,
                      W0Q.ap()[0:prow, 0:na], w0q_R, qsA)]
            if ncols > na:
                calls.append((prow, ncols - na,
                              W0.ap()[0:prow, na:ncols], w0_R,
                              W0Q.ap()[0:prow, na:ncols], w0q_R, qsB))
            emit_quant_multi(kb, calls)
            ntile = ncols // 128
            for base in range(0, ntile, 8):
                nn = min(8, ntile - base)

                def tr(h, base=base, nn=nn, prow=prow):
                    last = None
                    for j in range(nn):
                        last = h.transpose(
                            pst.ap()[:, j * 128:j * 128 + prow],
                            W0Q.ap()[0:prow,
                                     (base + j) * 128:(base + j + 1) * 128],
                            ident[0:prow, 0:prow])
                    return last
                kb.stage("tensor", tr, reads=[w0q_R, oi_R], writes=[pst_R[0]])

                def ev(s, nn=nn, prow=prow):
                    return s.copy(
                        W0S.ap()[:, 0:nn * prow]
                        .rearrange("p (j q) -> p j q", q=prow),
                        pst.ap()[:, 0:nn * 128]
                        .rearrange("p (j q) -> p j q", q=128)[:, :, 0:prow])
                kb.stage("scalar", ev, reads=[pst_R[0]], writes=[w0s_R])

                def out_dma(h, base=base, nn=nn, prow=prow):
                    dest = _dram_view(
                        wsh_in,
                        flat_base + (col0 + base * 128) * stride_i + shard_col_off,
                        [(stride_i, 128), (128 * stride_i, nn), (1, prow)])
                    return h.dma_start(
                        out=dest,
                        in_=W0S.ap()[:, 0:nn * prow]
                        .rearrange("p (j q) -> p j q", q=prow))
                kb.stage("sync", out_dma, reads=[w0s_R], writes=[wsh_R],
                         dma=True)


        shard_thunks = []
        if not lite0:
            for off0 in range(0, SH_GU, 128):
                prow = min(128, SH_GU - off0)
                shard_thunks.append(
                    lambda off0=off0, prow=prow: shard_tile(
                        wgate, off0, prow, 0, 1024, 0, SH_GU, off0))
                shard_thunks.append(
                    lambda off0=off0, prow=prow: shard_tile(
                        wup, off0, prow, 0, 1024, WU_OFF, SH_GU, off0))
            for (c0, ncols) in [(0, 1024), (1024, 1024), (2048, 768)]:
                shard_thunks.append(
                    lambda c0=c0, ncols=ncols: shard_tile(
                        wdown, 0, 128, c0, ncols, WD_OFF, SH_D, 0))
            shard_thunks.append(lambda: kb.stage(
                "gpsimd", lambda g: g.collective_compute(
                    "AllGather", ALU.bypass,
                    replica_groups=[list(range(NCORE))],
                    ins=[wsh_in.ap().opt()], outs=[wsh_all.ap().opt()]),
                reads=[wsh_R], writes=[wshall_R], cc=True))


        if stop_after < 1:
            kb.barrier()
            kb.finalize()
            return nc, dbg_outs
        kb.stage("vector", lambda v: v.memset(P_v, 0.0), writes=P_R)
        retscr_R = [R(f"retd{c}") for c in range(NCH)]
        qdscr_R = [R(f"qdd{c}") for c in range(NCH)]
        xfscr_R = [R(f"xfd{c}") for c in range(NCH)]
        x2qscr_R = [R(f"x2qd{c}") for c in range(NCH)]
        x2scr_R = [R(f"x2d{c}") for c in range(NCH)]
        sgscr_R = [R(f"sgd{c}") for c in range(NCH)]

        tmp1, tmp1R = ssv["tmp1"]
        tmp2, tmp2R = ssv["tmp2"]
        stm, stmR = ssv["stm"]
        stv, stvR = ssv["stv"]
        smq, smqR = ssv["smsq"]

        def hv(ap, half):
            return ap.rearrange("p (g s) -> p g s", s=128)[:, :, half * 64:(half + 1) * 64]

        def t3(ap):
            return ap.rearrange("p (g s) -> p g s", s=64)

        def cs_view(pc, part):
            return cs_v[pc][:, part * 512:(part + 1) * 512] \
                .rearrange("p (g s) -> p g s", s=64)

        def p1A_ld(c):
            pc = c % 2
            kb.stage("gpsimd", lambda h, c=c, pc=pc: h.dma_start(
                out=xb_v[pc], in_=xsl.ap()[c * CH:(c + 1) * CH, :]),
                writes=[xb_R[pc]], dma=True)

        def p1A(c):
            pc = c % 2
            kb.stage("sync", lambda h, c=c, pc=pc: h.dma_start(
                out=cs_v[pc], in_=cs_in.ap()[c * CH:(c + 1) * CH, :]),
                writes=[cs_R[pc]], dma=True)
            st = stm
            kb.stage("scalar", lambda a, pc=pc: a.activation(
                out=xfm_v, in_=xb_v[pc], func=AF.Square,
                accum_out=st[:, 0:1]),
                reads=[xb_R[pc]], writes=[xfm_R, stmR])
            kb.stage("scalar", lambda a, pc=pc: a.activation(
                out=xfm_v, in_=xb_v[pc], func=AF.Copy,
                accum_out=st[:, 1:2]),
                reads=[xb_R[pc]], writes=[xfm_R, stmR])
            kb.stage("vector", lambda v: v.tensor_scalar(
                st[:, 2:3], st[:, 1:2], 1.0 / 1024.0, None, ALU.mult),
                reads=[stmR], writes=[stmR])
            kb.stage("vector", lambda v: v.tensor_scalar(
                st[:, 3:4], st[:, 0:1], 1.0 / 1024.0, LN_EPS,
                ALU.mult, ALU.add),
                reads=[stmR], writes=[stmR])
            kb.stage("vector", lambda v: v.tensor_tensor(
                st[:, 4:5], st[:, 2:3], st[:, 2:3], ALU.mult),
                reads=[stmR], writes=[stmR])
            kb.stage("vector", lambda v: v.tensor_tensor(
                st[:, 3:4], st[:, 3:4], st[:, 4:5], ALU.subtract),
                reads=[stmR], writes=[stmR])
            emit_rsqrt(kb, st[:, 3:4], stmR, st[:, 8:9], stmR,
                       st[:, 6:7], stmR, st[:, 7:8], stmR, mg_v, mg_R, 1)
            kb.stage("vector", lambda v: v.scalar_tensor_tensor(
                st[:, 9:10], st[:, 2:3], -1.0, st[:, 8:9],
                ALU.mult, ALU.mult),
                reads=[stmR], writes=[stmR])
            kb.stage("vector", lambda v: v.tensor_copy(
                rstd_v[:, 0:1], st[:, 9:10]),
                reads=[stmR], writes=[rstd_R])
            kb.stage("tensor", lambda h: h.transpose(
                pst.ap()[0:1, 1024:1152], rstd_v[:, 0:1], ident),
                reads=[rstd_R, oi_R], writes=[pst_R[1]])
            kb.stage("scalar", lambda a, c=c: a.copy(
                NB2.ap()[0:1, c * 128:(c + 1) * 128],
                pst.ap()[0:1, 1024:1152]),
                reads=[pst_R[1]], writes=[nb2_R])
            kb.stage("vector", lambda v, pc=pc: v.tensor_scalar(
                xsq_v, xb_v[pc], st[:, 8:9], None, ALU.mult),
                reads=[xb_R[pc], stmR], writes=[xsq_R])

        def p1A2(c):
            pc = c % 2

            def trx(h, pc=pc):
                last = None
                for j in range(8):
                    last = h.transpose(pst.ap()[:, j * 128:(j + 1) * 128],
                                       xsq_v[:, j * 128:(j + 1) * 128],
                                       ident)
                return last
            kb.stage("tensor", trx, reads=[xsq_R, oi_R], writes=[pst_R[0]])
            kb.stage("scalar", lambda s, pc=pc: s.copy(
                xfc_v[pc], pst.ap()[:, 0:1024]),
                reads=[pst_R[0]], writes=[xfc_R[pc]])
            kb.stage("sync", lambda h, c=c, pc=pc: h.dma_start(
                out=xfms_scr.ap()[c * CH:(c + 1) * CH, :], in_=xfc_v[pc]),
                reads=[xfc_R[pc]], writes=[xfscr_R[c]], dma=True)
        def p1B(c):
            pc = c % 2

            def qkv_nt(h, nt, out_ap, c, pc, w_t, cbase):
                last = None
                for kc in range(8):
                    last = h.matmul(out_ap,
                                    xfc_v[pc][:, kc * 128:(kc + 1) * 128],
                                    w_t[kc][:, nt * 512:(nt + 1) * 512],
                                    start=(kc == 0), stop=False)
                last = h.matmul(out_ap, NB2.ap()[0:2, c * 128:(c + 1) * 128],
                                CB2.ap()[0:2, cbase + nt * 512:cbase + (nt + 1) * 512],
                                start=False, stop=True)
                return last

            kb.stage("tensor", lambda h, c=c, pc=pc: (
                qkv_nt(h, 0, psq.ap()[:, 0:512], c, pc, wqkv_t, 0),
                qkv_nt(h, 1, psq.ap()[:, 512:1024], c, pc, wqkv_t, 0))[-1],
                reads=[xfc_R[pc], wqkv_R, nb2_R, cb2_R],
                writes=psq_R)
            kb.stage("scalar", lambda s: s.copy(qsb_v, psq.ap()[:, :]),
                     reads=psq_R, writes=[qsb_R])
            kb.stage("tensor", lambda h, c=c, pc=pc: (
                qkv_nt(h, 2, psk.ap()[:, 0:512], c, pc, wqkv_t, 0),
                qkv_nt(h, 3, psk.ap()[:, 512:1024], c, pc, wqkv_t, 0))[-1],
                reads=[xfc_R[pc], wqkv_R, nb2_R, cb2_R],
                writes=psk_R)
            kb.stage("scalar", lambda s: s.copy(ksb_v, psk.ap()[:, :]),
                     reads=psk_R, writes=[ksb_R])
            for i, nt in enumerate(range(4, 8)):
                pv = [ps0, ps1][i % 2]
                pvR = [ps0_R, ps1_R][i % 2]
                kb.stage("tensor", lambda h, nt=nt, pv=pv, c=c, pc=pc:
                         qkv_nt(h, nt, pv.ap()[:, 0:512], c, pc, wqkv_t, 0),
                         reads=[xfc_R[pc], wqkv_R, nb2_R, cb2_R],
                         writes=[pvR])
                kb.stage("scalar", lambda s, i=i, pv=pv: s.copy(
                    vc_v[:, i * 512:(i + 1) * 512], pv.ap()[:, 0:512]),
                    reads=[pvR], writes=[vc_R])

            for (src_v, src_R, dst_v, dst_R, cpart, spart) in [
                    (qsb_v, qsb_R, qpr_v, qpr_R, 0, 1),
                    (ksb_v, ksb_R, kdc_v, kdc_R, 2, 3)]:
                kb.stage("gpsimd", lambda g, src_v=src_v, src_R=src_R, pc=pc,
                         cpart=cpart:
                         g.tensor_tensor(t3(tmp1), hv(src_v, 0),
                                         cs_view(pc, cpart), ALU.mult),
                         reads=[src_R, cs_R[pc]], writes=[tmp1R])
                kb.stage("gpsimd", lambda g, src_v=src_v, src_R=src_R, pc=pc,
                         spart=spart:
                         g.tensor_tensor(t3(tmp2), hv(src_v, 1),
                                         cs_view(pc, spart), ALU.mult),
                         reads=[src_R, cs_R[pc]], writes=[tmp2R])
                kb.stage("vector", lambda v, dst_v=dst_v, dst_R=dst_R:
                         v.tensor_tensor(hv(dst_v, 0), t3(tmp1), t3(tmp2),
                                         ALU.subtract),
                         reads=[tmp1R, tmp2R], writes=[dst_R])
                kb.stage("gpsimd", lambda g, src_v=src_v, src_R=src_R, pc=pc,
                         cpart=cpart:
                         g.tensor_tensor(t3(tmp1), hv(src_v, 1),
                                         cs_view(pc, cpart), ALU.mult),
                         reads=[src_R, cs_R[pc]], writes=[tmp1R])
                kb.stage("gpsimd", lambda g, src_v=src_v, src_R=src_R, pc=pc,
                         spart=spart:
                         g.tensor_tensor(t3(tmp2), hv(src_v, 0),
                                         cs_view(pc, spart), ALU.mult),
                         reads=[src_R, cs_R[pc]], writes=[tmp2R])
                kb.stage("vector", lambda v, dst_v=dst_v, dst_R=dst_R:
                         v.tensor_tensor(hv(dst_v, 1), t3(tmp1), t3(tmp2),
                                         ALU.add),
                         reads=[tmp1R, tmp2R], writes=[dst_R])

            kb.stage("vector", lambda v: v.tensor_tensor(
                qdt_v, qpr_v, QDD.ap()[:, :], ALU.mult),
                reads=[qpr_R, qdd_R], writes=[qdt_R])
        def p1C(c):
            pc = c % 2

            for (src_v, src_R, dst_v, dst_R, half, alias) in [
                    (qpr_v, qpr_R, qfm_v, qfm_R, 0, None),
                    (qdt_v, qdt_R, qdstg_v, qdstg_R, 1, ksb_R),
                    (kdc_v, kdc_R, kfm_v, kfm_R, 0, None)]:
                def trf(h, src_v=src_v, half=half):
                    last = None
                    for j in range(8):
                        last = h.transpose(
                            pst.ap()[:, half * 1024 + j * 128:
                                     half * 1024 + (j + 1) * 128],
                            src_v[:, j * 128:(j + 1) * 128], ident)
                    return last
                kb.stage("tensor", trf, reads=[src_R, oi_R],
                         writes=[pst_R[half]])
                wl = [dst_R] if alias is None else [dst_R, alias]
                kb.stage("scalar", lambda s, dst_v=dst_v, half=half: s.copy(
                    dst_v, pst.ap()[:, half * 1024:(half + 1) * 1024]),
                    reads=[pst_R[half]], writes=wl)
            kb.stage("sync", lambda h, c=c: h.dma_start(
                out=qdt_scr.ap()[c * CH:(c + 1) * CH, :], in_=qdstg_v),
                reads=[qdstg_R], writes=[qdscr_R[c]], dma=True)

            def sc_mm(h):
                last = None
                for h8 in range(H):
                    last = h.matmul(psq.ap()[:, h8 * 128:(h8 + 1) * 128],
                                    kfm_v[:, h8 * 128:(h8 + 1) * 128],
                                    qfm_v[:, h8 * 128:(h8 + 1) * 128],
                                    start=True, stop=True)
                return last
            kb.stage("tensor", sc_mm, reads=[kfm_R, qfm_R], writes=psq_R)
            for i in range(2):
                kb.stage("vector", lambda v, i=i: v.tensor_tensor(
                    scr_v[:, i * 512:(i + 1) * 512],
                    psq.ap()[:, i * 512:(i + 1) * 512],
                    DTP.ap()[:, i * 512:(i + 1) * 512], ALU.mult),
                    reads=[psq_R[i], dtp_R],
                    writes=[scr_R, qsb_R] if i == 0 else [scr_R])

            for r in range(4):
                pv = [ps0, ps1][r % 2]
                pvR = [ps0_R, ps1_R][r % 2]
                hh = [2 * r, 2 * r + 1]

                def ret_mm(h, hh=hh, pv=pv):
                    last = None
                    for i, h8 in enumerate(hh):
                        cols = pv.ap()[:, i * 256:(i + 1) * 256]
                        last = h.matmul(cols,
                                        scr_v[:, h8 * 128:(h8 + 1) * 128],
                                        vc_v[:, h8 * 256:(h8 + 1) * 256],
                                        start=True, stop=False)
                        last = h.matmul(cols,
                                        qdstg_v[:, h8 * 128:(h8 + 1) * 128],
                                        P_v[:, h8 * 256:(h8 + 1) * 256],
                                        start=False, stop=True)
                    return last
                kb.stage("tensor", ret_mm,
                         reads=[scr_R, vc_R, qdstg_R, P_R[r // 2]],
                         writes=[pvR])
                kb.stage("scalar", lambda s, r=r, pv=pv: s.copy(
                    ret_v[:, r * 512:(r + 1) * 512], pv.ap()[:, 0:512]),
                    reads=[pvR], writes=[ret_R])
            kb.stage("sync", lambda h, c=c: h.dma_start(
                out=ret_scr.ap()[c * CH:(c + 1) * CH, :], in_=ret_v),
                reads=[ret_R], writes=[retscr_R[c]], dma=True)

            for sr in range(2):
                hh = [4 * sr + i for i in range(4)]

                def st_mm(h, hh=hh):
                    last = None
                    for i, h8 in enumerate(hh):
                        cols = psk.ap()[:, i * 256:(i + 1) * 256]
                        last = h.matmul(cols,
                                        GCI.ap()[:, h8 * 128:(h8 + 1) * 128],
                                        P_v[:, h8 * 256:(h8 + 1) * 256],
                                        start=True, stop=False)
                        last = h.matmul(cols,
                                        kdc_v[:, h8 * 128:(h8 + 1) * 128],
                                        vc_v[:, h8 * 256:(h8 + 1) * 256],
                                        start=False, stop=True)
                    return last
                kb.stage("tensor", st_mm,
                         reads=[gci_R, kdc_R, vc_R, P_R[sr]],
                         writes=psk_R)
                kb.stage("scalar", lambda s, sr=sr: s.copy(
                    P_half[sr], psk.ap()[:, :]),
                    reads=psk_R, writes=[P_R[sr]])

        NC1 = min(NCH, nch1)
        p1A_ld(0)
        for t in range(NC1 + 3):
            if t + 1 < NC1:
                p1A_ld(t + 1)
            if 0 <= t - 1 < NC1:
                p1A2(t - 1)
            if 0 <= t - 3 < NC1:
                p1C(t - 3)
            if 0 <= t - 2 < NC1:
                p1B(t - 2)
            if t < NC1:
                p1A(t)
            if 0 <= t - 3 < NC1 and shard_thunks:
                shard_thunks.pop(0)()

        while shard_thunks:
            shard_thunks.pop(0)()


        if stop_after < 1.5:
            kb.barrier()
            kb.finalize()
            return nc, dbg_outs
        kb.stage("gpsimd", lambda h: h.dma_start(out=st_in.ap()[:, :], in_=P_v),
                 reads=P_R, writes=[stin_R], dma=True)
        kb.stage("gpsimd", lambda g: g.collective_compute(
            "AllGather", ALU.bypass, replica_groups=[list(range(NCORE))],
            ins=[st_in.ap().opt()], outs=[st_all.ap().opt()]),
            reads=[stin_R], writes=[stall_R], cc=True)
        kb.stage("vector", lambda v: v.memset(sext_v, 0.0), writes=[sext_R])

        def seg0(c):
            pc = c % 2
            kb.stage("sync", lambda h, c=c, pc=pc: h.dma_start(
                out=xfc_v[pc], in_=xfms_scr.ap()[c * CH:(c + 1) * CH, :]),
                reads=[xfscr_R[c]], writes=[xfc_R[pc]], dma=True)
            for nt in range(4):
                pv = [psq, psk][nt % 2]
                pvR = [psq_R[0], psk_R[0], psq_R[1], psk_R[1]][nt]
                hoff = ((nt // 2) % 2) * 512

                def g_mm(h, nt=nt, pv=pv, hoff=hoff, c=c, pc=pc):
                    out = pv.ap()[:, hoff:hoff + 512]
                    last = None
                    for kc in range(8):
                        last = h.matmul(out,
                                        xfc_v[pc][:, kc * 128:(kc + 1) * 128],
                                        wg_t[kc][:, nt * 512:(nt + 1) * 512],
                                        start=(kc == 0), stop=False)
                    last = h.matmul(out, NB2.ap()[0:2, c * 128:(c + 1) * 128],
                                    CB2.ap()[0:2, 4096 + nt * 512:
                                             4096 + (nt + 1) * 512],
                                    start=False, stop=True)
                    return last
                kb.stage("tensor", g_mm,
                         reads=[xfc_R[pc], wg_R, nb2_R, cb2_R],
                         writes=[pvR])
                kb.stage("scalar", lambda s, nt=nt, pv=pv, hoff=hoff:
                         s.activation(
                             out=sgstg_v[:, nt * 512:(nt + 1) * 512],
                             in_=pv.ap()[:, hoff:hoff + 512], func=AF.Silu),
                         reads=[pvR], writes=[sgstg_R])
            kb.stage("sync", lambda h, c=c: h.dma_start(
                out=sg_scr.ap()[c * CH:(c + 1) * CH, :], in_=sgstg_v),
                reads=[sgstg_R], writes=[sgscr_R[c]], dma=True)

        for r in range(NCORE):
            kb.stage("sync", lambda h, r=r: h.dma_start(
                out=sgtmp_v, in_=st_all.ap()[r * 128:(r + 1) * 128, :]),
                reads=[stall_R, vc_R], writes=[sgtmp_R], dma=True)

            def comb(v, r=r):
                last = None
                for h8 in range(H):
                    last = v.scalar_tensor_tensor(
                        sext_v[:, h8 * 256:(h8 + 1) * 256],
                        sgtmp_v[:, h8 * 256:(h8 + 1) * 256],
                        COEF.ap()[:, r * 8 + h8:r * 8 + h8 + 1],
                        sext_v[:, h8 * 256:(h8 + 1) * 256],
                        ALU.mult, ALU.add)
                return last
            kb.stage("vector", comb, reads=[sgtmp_R, con_R, sext_R],
                     writes=[sext_R])

        kb.barrier()
        donors = [vc_R, P_R[0], P_R[1], ret_R, dtp_R, qdd_R, gci_R, w0_R,
                  w0q_R, w0s_R, qdt_R, sgtmp_R]
        for kc in range(16):
            kb.stage("gpsimd", lambda h, kc=kc: h.dma_start(
                out=wp_t[kc], in_=wp_in.ap()[kc * 128:(kc + 1) * 128, :]),
                writes=([wp_R] + donors) if kc == 0 else [wp_R], dma=True)

        if dbg:
            dR = R("dbgc")
            kb.stage("vector", lambda v: v.tensor_copy(g_v, sext_v),
                     reads=[sext_R], writes=[dR])
            kb.stage("sync", lambda h: h.dma_start(
                out=dbg_outs["d_sext"].ap()[:, :], in_=g_v),
                reads=[dR], dma=True)
            dR2 = R("dbgc2")
            kb.stage("vector", lambda v: v.tensor_copy(retp_v[0], P_v),
                     reads=P_R, writes=[dR2])
            kb.stage("sync", lambda h: h.dma_start(
                out=dbg_outs["d_P"].ap()[:, :], in_=retp_v[0]),
                reads=[dR2], dma=True)

        if stop_after < 2:
            kb.barrier()
            kb.finalize()
            return nc, dbg_outs

        gp = [g_v, f32v(A5, 0, 2048)]
        gpR = [g_R, R("g1")]
        gnp = [ssv["gn"][0], ssv["tmp2"][0][:, 0:64]]
        gnpR = [ssv["gn"][1], R("gn1")]
        ln2p = [ssv["ln2"][0], ssv["tmp2"][0][:, 64:128]]
        ln2pR = [ssv["ln2"][1], R("ln21")]
        normp = [norm_v, bfv(A5, 4096, 2048)]
        normpR = [norm_R, R("norm1")]
        gatedp = [gated_v, bfv(A5, 8192, 2048)]
        gatedpR = [gated_R, R("gated1")]
        x2p = [x2_v, f32v(T1, 18432, 1024)]
        x2pR = [x2_R, R("x21")]
        x2qp = [x2q_v, bfv(A5, 10240, 1024)]
        x2qpR = [x2q_R, R("x2q1")]
        qdinp = [qdin_v, bfv(A5, 11264, 1024)]
        qdinpR = [qdin_R, R("qdin1")]
        gsqd = [ssv["tmp1"][0][:, 0:256], ssv["tmp1"][0][:, 256:512]]
        gsqdR = [ssv["tmp1"][1], R("gsqd1")]

        def seg1(c):
            pc = c % 2
            pc = c % 2
            kb.stage("sync", lambda h, c=c, pc=pc: [
                h.dma_start(out=retp_v[pc],
                            in_=ret_scr.ap()[c * CH:(c + 1) * CH, :]),
                h.dma_start(out=xc_v[pc],
                            in_=xsl.ap()[c * CH:(c + 1) * CH, :]),
                h.dma_start(out=sgin_v[pc],
                            in_=sg_scr.ap()[c * CH:(c + 1) * CH, :]),
                h.dma_start(out=qdinp[pc],
                            in_=qdt_scr.ap()[c * CH:(c + 1) * CH, :]),
            ], reads=[retscr_R[c], sgscr_R[c], qdscr_R[c]],
                writes=[retp_R[pc], xc_R[pc], sgin_R[pc], qdinpR[pc]], dma=True,
                n_dma=4)
            if c > 0:
                def sdecay(g):
                    last = None
                    for h8 in range(H):
                        last = g.tensor_scalar(
                            sext_v[:, h8 * 256:(h8 + 1) * 256],
                            sext_v[:, h8 * 256:(h8 + 1) * 256],
                            float(GAM_PY[h8] ** CH), None, ALU.mult)
                    return last
                kb.stage("vector", sdecay, reads=[sext_R], writes=[sext_R])
            for r in range(4):
                pv = [ps0, ps1][r % 2]
                pvR = [ps0_R, ps1_R][r % 2]
                hh = [2 * r, 2 * r + 1]

                def cr_mm(h, hh=hh, pv=pv, pc=pc):
                    last = None
                    for i, h8 in enumerate(hh):
                        last = h.matmul(pv.ap()[:, i * 256:(i + 1) * 256],
                                        qdinp[pc][:, h8 * 128:(h8 + 1) * 128],
                                        sext_v[:, h8 * 256:(h8 + 1) * 256],
                                        start=True, stop=True)
                    return last
                kb.stage("tensor", cr_mm, reads=[qdinpR[pc], sext_R], writes=[pvR])

                def gasm(v, hh=hh, pv=pv, pc=pc):
                    last = None
                    for i, h8 in enumerate(hh):
                        last = v.scalar_tensor_tensor(
                            gp[pc][:, h8 * 256:(h8 + 1) * 256],
                            pv.ap()[:, i * 256:(i + 1) * 256], 1.0,
                            retp_v[pc][:, h8 * 256:(h8 + 1) * 256],
                            ALU.mult, ALU.add,
                            accum_out=gnp[pc][:, h8:h8 + 1])
                    return last
                kb.stage("vector", gasm, reads=[pvR, retp_R[pc], gnpR[pc]],
                         writes=[gpR[pc], gnpR[pc]])
        def seg2h(c):
            pc = c % 2
            for h8 in range(H):
                kb.stage("scalar", lambda a, h8=h8, pc=pc: a.activation(
                    out=gsqd[pc][:, 0:256],
                    in_=gp[pc][:, h8 * 256:(h8 + 1) * 256],
                    func=AF.Square, accum_out=gnp[pc][:, 32 + h8:33 + h8]),
                    reads=[gpR[pc]], writes=[gsqdR[pc], gnpR[pc]])

        def seg2v(c, q):
            pc = c % 2
            q.stage("vector", lambda v, pc=pc: v.tensor_scalar(
                gnp[pc][:, 8:16], gnp[pc][:, 0:8], 1.0 / 256.0, None, ALU.mult),
                reads=[gnpR[pc]], writes=[gnpR[pc]])
            q.stage("vector", lambda v, pc=pc: v.tensor_scalar(
                gnp[pc][:, 40:48], gnp[pc][:, 32:40], 1.0 / 256.0, GN_EPS,
                ALU.mult, ALU.add),
                reads=[gnpR[pc]], writes=[gnpR[pc]])
            q.stage("vector", lambda v, pc=pc: v.tensor_tensor(
                gnp[pc][:, 16:24], gnp[pc][:, 8:16], gnp[pc][:, 8:16], ALU.mult),
                reads=[gnpR[pc]], writes=[gnpR[pc]])
            q.stage("vector", lambda v, pc=pc: v.tensor_tensor(
                gnp[pc][:, 40:48], gnp[pc][:, 40:48], gnp[pc][:, 16:24], ALU.subtract),
                reads=[gnpR[pc]], writes=[gnpR[pc]])
            emit_rsqrt(q, gnp[pc][:, 40:48], gnpR[pc], gnp[pc][:, 40:48], gnpR[pc],
                       gnp[pc][:, 48:56], gnpR[pc], gnp[pc][:, 56:64], gnpR[pc], mg_v, mg_R, 8)
            q.stage("vector", lambda v, pc=pc: v.scalar_tensor_tensor(
                gnp[pc][:, 24:32], gnp[pc][:, 8:16], -1.0, gnp[pc][:, 40:48],
                ALU.mult, ALU.mult),
                reads=[gnpR[pc]], writes=[gnpR[pc]])
            for h8 in range(H):
                q.stage("vector", lambda v, pc=pc, h8=h8: v.tensor_scalar(
                    normp[pc][:, h8 * 256:(h8 + 1) * 256],
                    gp[pc][:, h8 * 256:(h8 + 1) * 256],
                    gnp[pc][:, 40 + h8:41 + h8],
                    gnp[pc][:, 24 + h8:25 + h8], ALU.mult, ALU.add),
                    reads=[gpR[pc], gnpR[pc]], writes=[normpR[pc]])
            q.stage("vector", lambda v, pc=pc: v.tensor_tensor(
                gatedp[pc], normp[pc], sgin_v[pc], ALU.mult),
                reads=[normpR[pc], sgin_R[pc]], writes=[gatedpR[pc]])
        def seg3a(c):
            pc = c % 2
            for half in range(2):
                def trg(h, half=half, pc=pc):
                    last = None
                    for j in range(8):
                        last = h.transpose(
                            pst.ap()[:, half * 1024 + j * 128:
                                     half * 1024 + (j + 1) * 128],
                            gatedp[pc][:, half * 1024 + j * 128:
                                    half * 1024 + (j + 1) * 128],
                            ident)
                    return last
                kb.stage("tensor", trg, reads=[gatedpR[pc], oi_R],
                         writes=[pst_R[half]])
                kb.stage("scalar", lambda s, half=half: s.copy(
                    gatT_v[:, half * 1024:(half + 1) * 1024],
                    pst.ap()[:, half * 1024:(half + 1) * 1024]),
                    reads=[pst_R[half]], writes=[gatT_R])

        def seg3b(c):
            pc = c % 2
            for nt in range(2):
                def p_mm(h, nt=nt, c=c):
                    out = psq.ap()[:, nt * 512:(nt + 1) * 512]
                    last = None
                    for jv in range(16):
                        last = h.matmul(out,
                                        gatT_v[:, jv * 128:(jv + 1) * 128],
                                        wp_t[jv][:, nt * 512:(nt + 1) * 512],
                                        start=(jv == 0), stop=False)
                    last = h.matmul(out, NB2.ap()[0:2, c * 128:(c + 1) * 128],
                                    CB2.ap()[0:2, 6144 + nt * 512:
                                             6144 + (nt + 1) * 512],
                                    start=False, stop=True)
                    return last
                kb.stage("tensor", p_mm, reads=[gatT_R, wp_R, nb2_R, cb2_R],
                         writes=[psq_R[nt]])
                kb.stage("vector", lambda v, nt=nt, pc=pc:
                         v.scalar_tensor_tensor(
                             x2p[pc][:, nt * 512:(nt + 1) * 512],
                             psq.ap()[:, nt * 512:(nt + 1) * 512], 1.0,
                             xc_v[pc][:, nt * 512:(nt + 1) * 512],
                             ALU.mult, ALU.add,
                             accum_out=ln2p[pc][:, nt:nt + 1]),
                         reads=[psq_R[nt], xc_R[pc], ln2pR[pc]],
                         writes=[x2pR[pc], ln2pR[pc]])
            kb.stage("sync", lambda h, c=c, pc=pc: h.dma_start(
                out=x2_scr.ap()[c * CH:(c + 1) * CH, :], in_=x2p[pc]),
                reads=[x2pR[pc]], writes=[x2scr_R[c]], dma=True)
            kb.stage("scalar", lambda a, pc=pc: a.activation(
                out=qsap["ax"][:, 0:1024], in_=x2p[pc], func=AF.Square,
                accum_out=ln2p[pc][:, 5:6]),
                reads=[x2pR[pc]],
                writes=[qsA["ax"][1], qsB["ax"][1], ln2pR[pc]])

        def seg3v(c, q):
            pc = c % 2
            q.stage("vector", lambda v, pc=pc: v.tensor_tensor(
                ln2p[pc][:, 4:5], ln2p[pc][:, 0:1], ln2p[pc][:, 1:2], ALU.add),
                reads=[ln2pR[pc]], writes=[ln2pR[pc]])
            q.stage("vector", lambda v, pc=pc: v.tensor_scalar(
                ln2p[pc][:, 6:7], ln2p[pc][:, 4:5], 1.0 / 1024.0, None, ALU.mult),
                reads=[ln2pR[pc]], writes=[ln2pR[pc]])
            q.stage("vector", lambda v, pc=pc: v.tensor_scalar(
                ln2p[pc][:, 7:8], ln2p[pc][:, 5:6], 1.0 / 1024.0, LN_EPS,
                ALU.mult, ALU.add),
                reads=[ln2pR[pc]], writes=[ln2pR[pc]])
            q.stage("vector", lambda v, pc=pc: v.tensor_tensor(
                ln2p[pc][:, 8:9], ln2p[pc][:, 6:7], ln2p[pc][:, 6:7], ALU.mult),
                reads=[ln2pR[pc]], writes=[ln2pR[pc]])
            q.stage("vector", lambda v, pc=pc: v.tensor_tensor(
                ln2p[pc][:, 7:8], ln2p[pc][:, 7:8], ln2p[pc][:, 8:9], ALU.subtract),
                reads=[ln2pR[pc]], writes=[ln2pR[pc]])
            emit_rsqrt(q, ln2p[pc][:, 7:8], ln2pR[pc], ln2p[pc][:, 7:8], ln2pR[pc],
                       ln2p[pc][:, 10:11], ln2pR[pc], ln2p[pc][:, 11:12], ln2pR[pc], mg_v, mg_R, 1)
            q.stage("vector", lambda v, pc=pc: v.scalar_tensor_tensor(
                ln2p[pc][:, 9:10], ln2p[pc][:, 6:7], -1.0, ln2p[pc][:, 7:8],
                ALU.mult, ALU.mult),
                reads=[ln2pR[pc]], writes=[ln2pR[pc]])
            q.stage("vector", lambda v, pc=pc: v.tensor_scalar(
                x2p[pc], x2p[pc], ln2p[pc][:, 7:8], ln2p[pc][:, 9:10], ALU.mult, ALU.add),
                reads=[x2pR[pc], ln2pR[pc]], writes=[x2pR[pc]])
        def seg4a(c):
            pc = c % 2
            emit_quant_multi(kb, [
                (128, 512, x2p[pc][:, 0:512], x2pR[pc],
                 x2qp[pc][:, 0:512], x2qpR[pc], qsA, ("q",), False, True),
                (128, 512, x2p[pc][:, 512:1024], x2pR[pc],
                 x2qp[pc][:, 512:1024], x2qpR[pc], qsB, ("q",), False,
                 True)])

        def seg4b(c):
            pc = c % 2

            def trq(h, pc=pc):
                last = None
                for j in range(8):
                    last = h.transpose(pst.ap()[:, j * 128:(j + 1) * 128],
                                       x2qp[pc][:, j * 128:(j + 1) * 128], ident)
                return last
            kb.stage("tensor", trq, reads=[x2qpR[pc], oi_R], writes=[pst_R[0]])
            kb.stage("scalar", lambda s, pc=pc: s.copy(gatedp[pc][:, 0:1024],
                                                pst.ap()[:, 0:1024]),
                     reads=[pst_R[0]], writes=[gatedpR[pc]])
            kb.stage("sync", lambda h, c=c, pc=pc: h.dma_start(
                out=x2qt_scr.ap()[c * CH:(c + 1) * CH, :],
                in_=gatedp[pc][:, 0:1024]),
                reads=[gatedpR[pc]], writes=[x2qscr_R[c]], dma=True)


        NC2 = min(NCH, nch1)
        for t in range(NC2 + 7):
            c5, c4, c3, c2, c1, c0 = t - 6, t - 5, t - 4, t - 3, t - 2, t
            if 0 <= c4 < NC2:
                seg4a(c4)
            if 0 <= c3 < NC2:
                seg3a(c3)
                seg3b(c3)
            if 0 <= c5 < NC2:
                seg4b(c5)
            if 0 <= c2 < NC2:
                seg2h(c2)
            if 0 <= c3 < NC2:
                seg3v(c3, kb)
            if 0 <= c2 < NC2:
                seg2v(c2, kb)
            if 0 <= c1 < NC2:
                seg1(c1)
            if 0 <= c0 < NC2:
                seg0(c0)


        slot_ap = [psq.ap()[:, 0:512], psq.ap()[:, 512:1024],
                   psk.ap()[:, 0:512], psk.ap()[:, 512:1024],
                   ps0.ap()[:, 0:512], ps1.ap()[:, 0:512]]
        slotR = [R(f"sl{i}") for i in range(6)]
        hasm_v, hasm_R = W0.ap()[:, 0:1408], R("hasm")
        yst_R = R("ystore")

        hasm2_v = [W0.ap()[:, 0:1408], f32v(A5, 8192, 1408)]
        hasm2_R = [R("hasm0"), R("hasm1")]
        pending_q = []

        def ldw_gu(h, rh, woff):
            for r4 in range(4):
                r = rh * 4 + r4
                src = _dram_view(
                    wsh_all, r * SHARD_ELEMS + woff,
                    [(SH_GU, 128), (128 * SH_GU, 8), (1, SH_GU)])
                h.dma_start(
                    out=wf_v[:, r4 * 2816:(r4 + 1) * 2816]
                    .rearrange("p (kc j) -> p kc j", j=352),
                    in_=src)

        def emit_G(tc, rh):
            for r4 in range(4):
                sl = (tc * 4 + r4) % 6

                def gmm(h, tc=tc, r4=r4, sl=sl):
                    out = slot_ap[sl][:, 0:352]
                    last = None
                    for kc in range(8):
                        last = h.matmul(
                            out,
                            x2qt_v[:, tc * 1024 + kc * 128:
                                   tc * 1024 + (kc + 1) * 128],
                            wf_v[:, r4 * 2816 + kc * 352:
                                 r4 * 2816 + (kc + 1) * 352],
                            start=(kc == 0), stop=(kc == 7))
                    return last
                kb.stage("tensor", gmm, reads=[x2qt_R, wf_R],
                         writes=[slotR[sl]])
                idx = tc * 1408 + r4 * 352
                kb.stage("scalar", lambda s, sl=sl, idx=idx:
                         s.activation(out=sgb_v[:, idx:idx + 352],
                                      in_=slot_ap[sl][:, 0:352],
                                      func=AF.Silu),
                         reads=[slotR[sl]], writes=[sgb_R])

        def emit_U(tc, rh):
            hp = tc % 2
            for r4 in range(4):
                sl = (tc * 4 + r4) % 6

                def umm(h, tc=tc, r4=r4, sl=sl):
                    out = slot_ap[sl][:, 0:352]
                    last = None
                    for kc in range(8):
                        last = h.matmul(
                            out,
                            x2qt_v[:, tc * 1024 + kc * 128:
                                   tc * 1024 + (kc + 1) * 128],
                            wf_v[:, r4 * 2816 + kc * 352:
                                 r4 * 2816 + (kc + 1) * 352],
                            start=(kc == 0), stop=(kc == 7))
                    return last
                kb.stage("tensor", umm, reads=[x2qt_R, wf_R],
                         writes=[slotR[sl]])
                idx = tc * 1408 + r4 * 352
                kb.stage("vector", lambda v, sl=sl, idx=idx, r4=r4, hp=hp:
                         v.tensor_tensor(
                             hasm2_v[hp][:, r4 * 352:(r4 + 1) * 352],
                             slot_ap[sl][:, 0:352],
                             sgb_v[:, idx:idx + 352], ALU.mult),
                         reads=[slotR[sl], sgb_R], writes=[hasm2_R[hp]])

        def emit_Qa(tc, rh):
            hp = tc % 2
            hq_t = hq_v[hp]
            hqR = hq_R[hp]
            emit_quant_multi(kb, [
                (128, 512, hasm2_v[hp][:, 0:512], hasm2_R[hp],
                 hq_t[:, 0:512], hqR, qsA, ("ss", "q"), True),
                (128, 512, hasm2_v[hp][:, 512:1024], hasm2_R[hp],
                 hq_t[:, 512:1024], hqR, qsB, ("ss", "q"), True),
                (128, 384, hasm2_v[hp][:, 1024:1408], hasm2_R[hp],
                 hq_t[:, 1024:1408], hqR, qs3, ("ss", "q"), True)])

        def emit_Qb(tc, rh):
            hp = tc % 2
            hq_t = hq_v[hp]
            hqR = hq_R[hp]
            for base in range(0, 11, 8):
                nn = min(8, 11 - base)
                half = (base // 8) % 2

                def trh(h, base=base, nn=nn, half=half, hq_t=hq_t):
                    last = None
                    for j in range(nn):
                        last = h.transpose(
                            pst.ap()[:, half * 1024 + j * 128:
                                     half * 1024 + (j + 1) * 128],
                            hq_t[:, (base + j) * 128:(base + j + 1) * 128],
                            ident)
                    return last
                kb.stage("tensor", trh, reads=[hqR, oi_R],
                         writes=[pst_R[half]])
                base_col = rh * 11264 + base * 1024 + tc * 128

                def evh(s, base_col=base_col, nn=nn, half=half):
                    dst = hqt[:, base_col:base_col + nn * 1024] \
                        .rearrange("p (j q) -> p j q", q=1024)[:, :, 0:128]
                    return s.copy(
                        dst,
                        pst.ap()[:, half * 1024:half * 1024 + nn * 128]
                        .rearrange("p (j q) -> p j q", q=128))
                kb.stage("scalar", evh, reads=[pst_R[half]], writes=[hqt_R])

        for th in range(2):
            kb.stage("sync", lambda h, th=th: h.dma_start(
                out=x2qt_v.rearrange("p (t f) -> p t f", f=1024),
                in_=x2qt_scr.ap()[th * 1024:(th + 1) * 1024, :]
                .rearrange("(t p) f -> p t f", p=128)),
                reads=[x2qscr_R[th * 8 + i] for i in range(8)],
                writes=[x2qt_R], dma=True)
            for rh in range(2):
                kb.stage("sync", lambda h, rh=rh: ldw_gu(h, rh, 0),
                         reads=[wshall_R], writes=[wf_R], dma=True, n_dma=4)
                for tc in range(8):
                    emit_G(tc, rh)
                    if tc == 0 and pending_q:
                        pending_q.pop(0)()
                kb.stage("sync", lambda h, rh=rh: ldw_gu(h, rh, WU_OFF),
                         reads=[wshall_R], writes=[wf_R], dma=True, n_dma=4)
                for tc in range(8):
                    emit_U(tc, rh)
                    if tc > 0:
                        emit_Qa(tc - 1, rh)
                    if tc > 1:
                        emit_Qb(tc - 2, rh)

                def q_tail(rh=rh):
                    emit_Qa(7, rh)
                    emit_Qb(6, rh)
                    emit_Qb(7, rh)
                pending_q.append(q_tail)
            while pending_q:
                pending_q.pop(0)()
            for nt in range(2):
                def ldwd(h, nt=nt):
                    for r4 in range(4):
                        r = nt * 4 + r4
                        src = _dram_view(
                            wsh_all, r * SHARD_ELEMS + WD_OFF,
                            [(SH_D, 128), (128 * SH_D, 22), (1, 128)])
                        dst = wf_v[:, 0:11264] \
                            .rearrange("p (kc q) -> p kc q", q=512) \
                            [:, :, r4 * 128:(r4 + 1) * 128]
                        h.dma_start(out=dst, in_=src)
                kb.stage("sync", ldwd, reads=[wshall_R], writes=[wf_R],
                         dma=True, n_dma=4)
                for tc in range(8):
                    sl = [0, 2, 4, 5][tc % 4]

                    def dmm(h, tc=tc, sl=sl):
                        out = slot_ap[sl]
                        last = None
                        for kc in range(22):
                            last = h.matmul(
                                out,
                                hqt[:, kc * 1024 + tc * 128:
                                    kc * 1024 + (tc + 1) * 128],
                                wf_v[:, kc * 512:(kc + 1) * 512],
                                start=(kc == 0), stop=(kc == 21))
                        return last
                    kb.stage("tensor", dmm, reads=[hqt_R, wf_R],
                             writes=[slotR[sl]])
                    pcc = tc % 2
                    kb.stage("sync", lambda h, th=th, tc=tc, pcc=pcc:
                             h.dma_start(
                                 out=x2r_v[pcc],
                                 in_=x2_scr.ap()[th * 1024 + tc * 128:
                                                 th * 1024 + (tc + 1) * 128, :]),
                             reads=[x2scr_R[th * 8 + tc]],
                             writes=[x2r_R[pcc]], dma=True)
                    kb.stage("vector", lambda v, sl=sl, pcc=pcc, nt=nt:
                             v.scalar_tensor_tensor(
                                 y_v[:, nt * 512:(nt + 1) * 512],
                                 slot_ap[sl], 1.0,
                                 x2r_v[pcc][:, nt * 512:(nt + 1) * 512],
                                 ALU.mult, ALU.add),
                             reads=[slotR[sl], x2r_R[pcc]], writes=[y_R])
                    kb.stage("sync", lambda h, th=th, tc=tc, nt=nt:
                             h.dma_start(
                                 out=y_out.ap()[th * 1024 + tc * 128:
                                                th * 1024 + (tc + 1) * 128,
                                                nt * 512:(nt + 1) * 512],
                                 in_=y_v[:, nt * 512:(nt + 1) * 512]),
                             reads=[y_R], writes=[yst_R], dma=True)
        kb.barrier()
        kb.finalize()
    return nc, dbg_outs


def make_in_maps(inputs):
    x = np.ascontiguousarray(
        np.asarray(inputs["L0_attn_hidden_states"], dtype=np.float32))
    consts = _host_consts()
    gam = consts["gam"]
    perm_block = _perm_qk()
    perm = np.concatenate([perm_block, 1024 + perm_block,
                           2048 + np.arange(2048)])
    wqkvp = np.ascontiguousarray(
        np.asarray(inputs["L0_attn_w_qkv"], np.float32)[:, perm])
    b_qkv_p = np.asarray(inputs["L0_attn_b_qkv"], np.float32)[perm]
    biases = np.ascontiguousarray(np.concatenate(
        [b_qkv_p, np.asarray(inputs["L0_attn_b_g"], np.float32),
         np.asarray(inputs["L0_attn_b_p"], np.float32)])[None, :])
    wg = np.ascontiguousarray(np.asarray(inputs["L0_attn_w_g"], np.float32))
    wp = np.ascontiguousarray(np.asarray(inputs["L0_attn_w_p"], np.float32))
    wgate = np.ascontiguousarray(np.asarray(inputs["L0_ffn_w_gate"], np.float32))
    wup = np.ascontiguousarray(np.asarray(inputs["L0_ffn_w_up"], np.float32))
    wdown = np.ascontiguousarray(np.asarray(inputs["L0_ffn_w_down"], np.float32))
    wdown_t = np.ascontiguousarray(wdown.T)
    in_maps = []
    for core in range(NCORE):
        cc = _core_consts(core, gam)
        bi, t0 = cc["bi"], cc["t0"]
        r0 = core * SH_GU
        in_maps.append({
            "xsl": np.ascontiguousarray(x[bi, t0:t0 + TLOC, :]),
            "cs_in": np.ascontiguousarray(cc["cs"]),
            "wqkvp": wqkvp,
            "wg_in": wg,
            "wp_in": wp,
            "wgate": np.ascontiguousarray(wgate[r0:r0 + SH_GU, :]),
            "wup": np.ascontiguousarray(wup[r0:r0 + SH_GU, :]),
            "wdown": np.ascontiguousarray(
                wdown[core * SH_D:(core + 1) * SH_D, :]),
            "bia_in": biases,
            "dtp_in": np.ascontiguousarray(consts["dtp"]),
            "qdd_in": np.ascontiguousarray(consts["qddec"]),
            "gci_in": np.ascontiguousarray(consts["gci"]),
            "coef_in": np.ascontiguousarray(cc["coef"]),
            "oi_in": np.ascontiguousarray(consts["oi"]),
        })
    return in_maps


def kernel(**inputs):
    nc, _ = build_nc(dbg=False)
    in_maps = make_in_maps(inputs)
    res = run_bass_kernel_spmd(nc, in_maps, core_ids=list(range(NCORE)))
    y = np.empty((B, T, E), dtype=np.float32)
    for core in range(NCORE):
        bi, ci = core // 4, core % 4
        y[bi, ci * TLOC:(ci + 1) * TLOC, :] = res.results[core]["y"]
    return y
